# version 01
# baseline (speedup 1.0000x reference)
import numpy as np
import concourse.bass as bass
import concourse.mybir as mybir
from concourse.bass_utils import run_bass_kernel_spmd

F32 = mybir.dt.float32
AF = mybir.ActivationFunctionType
ALU = mybir.AluOpType
AX = mybir.AxisListType

D = 1024
KC = D // 128
NE = 12
DA = 64
PD = 256
TPC = 4
BIG = 1.0e30
INV_TAU = 1.0 / (1.0 + 1e-8)
LN_EPS = 1e-5
Z_EPS = 1e-8


class Sem:
    def __init__(self, nc, name):
        self.h = nc.semaphore(name).__enter__()
        self.cnt = 0

    def inc(self, instr, n=1):
        instr.then_inc(self.h, n)
        self.cnt += n
        return instr


class Builder:

    def __init__(self, nc, n_tok):
        self.nc = nc
        self.n_tok = n_tok
        self.n_tiles = n_tok // 128
        self.n_chunks = self.n_tiles // TPC
        t = {}
        self.t = t
        t["x"] = nc.dram_tensor("x", [n_tok, D], F32, kind="ExternalInput")
        t["o"] = nc.dram_tensor("o", [n_tok, D], F32, kind="ExternalOutput")
        t["w1t"] = nc.dram_tensor("w1t", [NE, D, D], F32, kind="ExternalInput")
        t["w2t"] = nc.dram_tensor("w2t", [NE, D, D], F32, kind="ExternalInput")
        t["pwt"] = nc.dram_tensor("pwt", [D, DA], F32, kind="ExternalInput")
        t["ut"] = nc.dram_tensor("ut", [DA, 3 * NE], F32, kind="ExternalInput")
        t["b2"] = nc.dram_tensor("b2", [NE, D], F32, kind="ExternalInput")
        t["qp"] = nc.dram_tensor("qp", [D, PD], F32, kind="ExternalInput")
        t["kp"] = nc.dram_tensor("kp", [D, PD], F32, kind="ExternalInput")
        t["ident"] = nc.dram_tensor("ident", [128, 128], F32, kind="ExternalInput")
        sb = nc.alloc_sbuf_tensor
        self.x_res = sb("x_res", [128, self.n_tiles * D], F32)
        self.A = sb("A", [128, TPC * D], F32)
        self.B = sb("B", [128, KC * 512], F32)
        self.wb = [sb("wb0", [128, 2048], F32), sb("wb1", [128, 2048], F32)]
        self.ident_s = sb("ident_s", [128, 128], F32)
        self.pwt_s = sb("pwt_s", [128, KC * DA], F32)
        self.ut_s = sb("ut_s", [DA, 3 * NE], F32)
        self.b2_s = sb("b2_s", [NE, D], F32)
        self.qp_s = sb("qp_s", [128, KC * PD], F32)
        self.kp_s = sb("kp_s", [128, KC * PD], F32)
        self.C1 = sb("C1", [128, TPC * NE], F32)
        self.C2 = sb("C2", [128, TPC * NE], F32)
        self.C3 = sb("C3", [128, TPC * NE], F32)
        self.C3T = sb("C3T", [NE, TPC * 128], F32)
        self.xa_s = sb("xa_s", [DA, TPC * 128], F32)
        self.z_s = sb("z_s", [128, TPC * 3 * NE], F32)
        self.mcol = sb("mcol", [128, self.n_tiles], F32)
        self.rcol = sb("rcol", [128, self.n_tiles], F32)
        self.zcol = sb("zcol", [128, TPC], F32)
        self.scr = sb("scr", [128, 16], F32)
        self.t12 = sb("t12", [128, 2 * NE], F32)
        self.ones = sb("ones", [128, 1], F32)
        self.pA = nc.alloc_psum_tensor("pA", [128, 2048], F32)
        self.pB = nc.alloc_psum_tensor("pB", [128, 2048], F32)
        self.sv = Sem(nc, "sv")
        self.ss = Sem(nc, "ss")
        self.st = Sem(nc, "st_t")
        self.sw = [Sem(nc, "sw0"), Sem(nc, "sw1")]
        self.sin = Sem(nc, "sin")
        self.sout = Sem(nc, "sout")
        self.wb_free_at = [0, 0]

    def wv(self, eng, val=None):
        eng.wait_ge(self.sv.h, self.sv.cnt if val is None else val)

    def ws(self, eng, val=None):
        eng.wait_ge(self.ss.h, self.ss.cnt if val is None else val)

    def wt(self, eng, val=None):
        eng.wait_ge(self.st.h, self.st.cnt if val is None else val)

    def emit_loads(self):
        nc, t = self.nc, self.t
        qtiles = self.n_tiles // 4
        for q in range(4):
            rows = t["x"][q * qtiles * 128:(q + 1) * qtiles * 128, :]
            src = rows.rearrange("(t p) f -> p t f", p=128)
            dst = self.x_res[:, q * qtiles * D:(q + 1) * qtiles * D]
            dst = dst.rearrange("p (t f) -> p t f", f=D)
            self.sin.inc(nc.sync.dma_start(out=dst, in_=src), 16)
        self.sin.inc(nc.sync.dma_start(out=self.ident_s[:], in_=t["ident"][:]), 16)
        pw = t["pwt"][:].rearrange("(kc p) j -> p kc j", p=128)
        self.sin.inc(nc.sync.dma_start(
            out=self.pwt_s[:].rearrange("p (kc j) -> p kc j", j=DA), in_=pw), 16)
        self.sin.inc(nc.sync.dma_start(out=self.ut_s[:], in_=t["ut"][:]), 16)
        self.sin.inc(nc.sync.dma_start(out=self.b2_s[:], in_=t["b2"][:]), 16)
        qpr = t["qp"][:].rearrange("(kc p) e -> p kc e", p=128)
        self.sin.inc(nc.sync.dma_start(
            out=self.qp_s[:].rearrange("p (kc e) -> p kc e", e=PD), in_=qpr), 16)
        kpr = t["kp"][:].rearrange("(kc p) e -> p kc e", p=128)
        self.sin.inc(nc.sync.dma_start(
            out=self.kp_s[:].rearrange("p (kc e) -> p kc e", e=PD), in_=kpr), 16)
        for eng in (nc.vector, nc.scalar, nc.tensor):
            eng.wait_ge(self.sin.h, self.sin.cnt)
        self.sv.inc(nc.vector.memset(self.ones[:], 1.0))

    def emit_ln(self, chunk, save_stats=False, use_saved=False):
        nc = self.nc
        V, S = nc.vector, nc.scalar
        for tt in range(TPC):
            g = chunk * TPC + tt
            xt = self.x_res[:, g * D:(g + 1) * D]
            At = self.A[:, tt * D:(tt + 1) * D]
            if use_saved:
                m = self.mcol[:, g:g + 1]
                r = self.rcol[:, g:g + 1]
                if tt == 0:
                    self.wt(V)
                    self.ws(V)
                    self.wv(V)
                self.sv.inc(V.tensor_scalar(At, xt, m, r, ALU.subtract, ALU.mult))
                continue
            m = self.mcol[:, g:g + 1] if save_stats else self.scr[:, 0:1]
            r = self.rcol[:, g:g + 1] if save_stats else self.scr[:, 1:2]
            sumc = self.scr[:, 2:3]
            sqc = self.scr[:, 3:4]
            varc = self.scr[:, 4:5]
            msq = self.scr[:, 5:6]
            if tt == 0:
                self.wv(V)
            self.sv.inc(V.reduce_sum(sumc, xt, axis=AX.X))
            if tt == 0:
                self.wt(S)
            self.wv(S)
            self.ss.inc(S.activation(At, xt, AF.Square, accum_out=sqc))
            self.ws(V)
            self.wv(V)
            self.sv.inc(V.tensor_scalar_mul(m, sumc, 1.0 / D))
            self.wv(V)
            self.sv.inc(V.tensor_mul(msq, m, m))
            self.sv.inc(V.tensor_scalar(varc, sqc, 1.0 / D, LN_EPS, ALU.mult, ALU.add))
            self.wv(V)
            self.sv.inc(V.tensor_sub(varc, varc, msq))
            self.wv(S)
            self.ss.inc(S.activation(varc, varc, AF.Ln))
            self.ws(S)
            self.ss.inc(S.activation(r, varc, AF.Exp, scale=-0.5))
            self.ws(V)
            self.sv.inc(V.tensor_scalar(At, xt, m, r, ALU.subtract, ALU.mult))

    def emit_transpose_A_to_B(self, arena, s0, s1):
        nc = self.nc
        T, S = nc.tensor, nc.scalar
        first = True
        for tt in range(TPC):
            for kc in range(KC):
                i = tt * KC + kc
                base = s0 if i % 2 == 0 else s1
                pview = arena[:, base: base + 128]
                src = self.A[:, tt * D + kc * 128: tt * D + (kc + 1) * 128]
                if first:
                    self.wv(T)
                    self.ws(T)
                    first = False
                else:
                    self.ws(T, self.ss.cnt - 1)
                self.st.inc(T.transpose(pview, src, self.ident_s[:]))
                self.wt(S)
                dst = self.B[:, kc * 512 + tt * 128: kc * 512 + (tt + 1) * 128]
                self.ss.inc(S.copy(dst, pview))

    def emit_routing(self, chunk):
        nc = self.nc
        T, V, S = nc.tensor, nc.vector, nc.scalar
        xap = self.pB[0:DA, 0:512]
        self.ws(T)
        self.wv(T)
        for kc in range(KC):
            ins = T.matmul(xap, self.pwt_s[:, kc * DA:(kc + 1) * DA],
                           self.B[:, kc * 512:(kc + 1) * 512],
                           start=(kc == 0), stop=(kc == KC - 1))
        self.st.inc(ins)
        self.wt(V)
        self.sv.inc(V.tensor_copy(self.xa_s[:], xap))
        zp = self.pB[:, 512:512 + TPC * 3 * NE]
        self.wv(T)
        for tt in range(TPC):
            for r in range(3):
                self.st.inc(T.matmul(zp[:, tt * 36 + r * NE: tt * 36 + (r + 1) * NE],
                                     self.xa_s[:, tt * 128:(tt + 1) * 128],
                                     self.ut_s[:, r * NE:(r + 1) * NE],
                                     start=True, stop=True))
        self.wt(V)
        self.sv.inc(V.tensor_copy(self.z_s[:], zp))
        for tt in range(TPC):
            for r in range(3):
                z = self.z_s[:, tt * 36 + r * NE: tt * 36 + (r + 1) * NE]
                C = (self.C1, self.C2, self.C3)[r]
                Ccols = C[:, tt * NE:(tt + 1) * NE]
                m1 = self.scr[:, 6:7]
                m3 = self.scr[:, 7:8]
                bia = self.scr[:, 8:9]
                den = self.scr[:, 9:10]
                e1 = self.t12[:, 0:NE]
                zt = self.t12[:, NE:2 * NE]
                self.wv(V)
                self.sv.inc(V.reduce_max(m1, z, axis=AX.X))
                self.wv(V)
                self.sv.inc(V.tensor_scalar(e1, z, m1, None, ALU.is_ge))
                self.wv(V)
                self.sv.inc(V.scalar_tensor_tensor(zt, e1, -BIG, z, ALU.mult, ALU.add))
                self.wv(V)
                self.sv.inc(V.reduce_max(m3, zt, axis=AX.X))
                self.wv(V)
                self.sv.inc(V.tensor_scalar(e1, zt, m3, None, ALU.is_ge))
                self.wv(V)
                self.sv.inc(V.scalar_tensor_tensor(zt, e1, -BIG, zt, ALU.mult, ALU.add))
                self.wv(V)
                self.sv.inc(V.reduce_max(m3, zt, axis=AX.X))
                self.sv.inc(V.tensor_scalar_mul(bia, m1, -INV_TAU))
                self.wv(S)
                self.ss.inc(S.activation(zt, z, AF.Exp, bias=bia, scale=INV_TAU))
                self.ws(V)
                self.wv(V)
                self.sv.inc(V.tensor_scalar(e1, z, m3, None, ALU.is_ge))
                self.wv(V)
                self.sv.inc(V.tensor_mul(zt, zt, e1))
                self.wv(V)
                self.sv.inc(V.reduce_sum(den, zt, axis=AX.X))
                self.wv(V)
                self.sv.inc(V.reciprocal(den, den))
                self.wv(V)
                if r == 0:
                    self.sv.inc(V.tensor_scalar(Ccols, zt, den, None, ALU.mult))
                else:
                    self.sv.inc(V.tensor_scalar(Ccols, zt, den, 0.5, ALU.mult, ALU.mult))
        for tt in range(TPC):
            b = 1024 if tt % 2 == 0 else 1536
            pv = self.pB[0:NE, b: b + 128]
            self.wv(T)
            self.ws(T)
            self.st.inc(T.transpose(pv, self.C3[:, tt * NE:(tt + 1) * NE], self.ident_s[:]))
            self.wt(S)
            self.ss.inc(S.copy(self.C3T[:, tt * 128:(tt + 1) * 128], pv))

    def emit_mixture(self, chunk, wdram, C, dest_is_xres, first_clears):
        nc = self.nc
        T, V = nc.tensor, nc.vector
        NSU = 4 * NE

        def load_su(u):
            e, oh, kb = u // 4, (u // 2) % 2, u % 2
            p = u % 2
            src = wdram[e, kb * 512:(kb + 1) * 512, oh * 512:(oh + 1) * 512]
            src = src.rearrange("(k p) o -> p k o", p=128)
            dst = self.wb[p][:].rearrange("p (k o) -> p k o", o=512)
            nc.sync.wait_ge(self.st.h, self.wb_free_at[p])
            self.sw[p].inc(nc.sync.dma_start(out=dst, in_=src), 16)

        self.wv(T)
        self.ws(T)
        sw_base = [self.sw[0].cnt, self.sw[1].cnt]
        load_su(0)
        load_su(1)
        evac_after = {}
        for u in range(NSU):
            e, oh, kb = u // 4, (u // 2) % 2, u % 2
            p = u % 2
            pair = u // 2
            grp = self.pA if pair % 2 == 0 else self.pB
            T.wait_ge(self.sw[p].h, sw_base[p] + 16 * (u // 2 + 1))
            if kb == 0 and pair >= 2:
                T.wait_ge(self.sv.h, evac_after[pair - 2])
            last = None
            for k4 in range(4):
                kc = kb * 4 + k4
                for tt in range(TPC):
                    last = T.matmul(grp[:, tt * 512:(tt + 1) * 512],
                                    self.B[:, kc * 512 + tt * 128: kc * 512 + (tt + 1) * 128],
                                    self.wb[p][:, k4 * 512:(k4 + 1) * 512],
                                    start=(kc == 0), stop=(kc == KC - 1))
            self.st.inc(last)
            self.wb_free_at[p] = self.st.cnt
            if u + 2 < NSU:
                load_su(u + 2)
            if kb == 1:
                self.wt(V)
                lastv = None
                for tt in range(TPC):
                    ps = grp[:, tt * 512:(tt + 1) * 512]
                    if dest_is_xres:
                        g = chunk * TPC + tt
                        dst = self.x_res[:, g * D + oh * 512: g * D + oh * 512 + 512]
                    else:
                        dst = self.A[:, tt * D + oh * 512: tt * D + oh * 512 + 512]
                    col = C[:, tt * NE + e: tt * NE + e + 1]
                    if first_clears and e == 0:
                        lastv = V.tensor_scalar(dst, ps, col, None, ALU.mult)
                    else:
                        lastv = V.scalar_tensor_tensor(dst, ps, col, dst, ALU.mult, ALU.add)
                self.sv.inc(lastv)
                evac_after[pair] = self.sv.cnt

    def emit_gelu(self):
        nc = self.nc
        self.wv(nc.scalar)
        self.ss.inc(nc.scalar.activation(self.A[:], self.A[:], AF.Gelu))

    def emit_b2_term(self, chunk):
        nc = self.nc
        T, V = nc.tensor, nc.vector
        self.wv(T)
        self.ws(T)
        for tt in range(TPC):
            for dh in range(2):
                pv = self.pA[:, ((tt * 2 + dh) % 4) * 512: ((tt * 2 + dh) % 4) * 512 + 512]
                if tt * 2 + dh >= 4:
                    self.wv(T)
                self.st.inc(T.matmul(pv, self.C3T[:, tt * 128:(tt + 1) * 128],
                                     self.b2_s[:, dh * 512:(dh + 1) * 512],
                                     start=True, stop=True))
                self.wt(V)
                g = chunk * TPC + tt
                dst = self.x_res[:, g * D + dh * 512: g * D + dh * 512 + 512]
                self.sv.inc(V.tensor_add(dst, dst, pv))

    def emit_ffn_chunk(self, chunk):
        self.emit_ln(chunk)
        self.emit_transpose_A_to_B(self.pA, 0, 512)
        self.emit_routing(chunk)
        self.emit_mixture(chunk, self.t["w1t"][:], self.C1,
                          dest_is_xres=False, first_clears=True)
        self.emit_gelu()
        self.emit_transpose_A_to_B(self.pA, 0, 512)
        self.emit_b2_term(chunk)
        self.emit_mixture(chunk, self.t["w2t"][:], self.C2,
                          dest_is_xres=True, first_clears=False)

    def emit_attn(self):
        nc = self.nc
        T, V, S = nc.tensor, nc.vector, nc.scalar
        kprime = self.wb[0][:, 0:TPC * PD]
        kv_s0 = self.wb[0][:, 1024:2048]
        kv_s1 = self.wb[1][:, 1024:2048]
        qprime = self.wb[1][:, 0:1024]
        for c in range(self.n_chunks):
            self.emit_ln(c, save_stats=True)
            self.emit_transpose_A_to_B(self.pB, 1024, 1536)
            self.ws(T)
            if c == 0:
                self.wt(S, self.st.cnt)
            for tt in range(TPC):
                pv = self.pB[:, tt * PD:(tt + 1) * PD]
                last = None
                for kc in range(KC):
                    last = T.matmul(pv, self.B[:, kc * 512 + tt * 128: kc * 512 + (tt + 1) * 128],
                                    self.kp_s[:, kc * PD:(kc + 1) * PD],
                                    start=(kc == 0), stop=(kc == KC - 1))
                self.st.inc(last)
            self.wt(S)
            for tt in range(TPC):
                self.ss.inc(S.activation(kprime[:, tt * PD:(tt + 1) * PD],
                                         self.pB[:, tt * PD:(tt + 1) * PD], AF.Exp))
            self.ws(T)
            last = None
            for ec in range(2):
                for dh in range(2):
                    pv = self.pA[:, (ec * 2 + dh) * 512:(ec * 2 + dh + 1) * 512]
                    for tt in range(TPC):
                        last = T.matmul(pv,
                                        kprime[:, tt * PD + ec * 128: tt * PD + ec * 128 + 128],
                                        self.A[:, tt * D + dh * 512: tt * D + dh * 512 + 512],
                                        start=(c == 0 and tt == 0),
                                        stop=(c == self.n_chunks - 1 and tt == TPC - 1))
            self.st.inc(last)
        self.wt(V)
        self.sv.inc(V.tensor_copy(kv_s0, self.pA[:, 0:1024]))
        self.sv.inc(V.tensor_copy(kv_s1, self.pA[:, 1024:2048]))
        for c in range(self.n_chunks):
            self.emit_ln(c, use_saved=True)
            self.emit_transpose_A_to_B(self.pA, 0, 512)
            self.ws(T)
            self.wv(T)
            for ec in range(2):
                pv = self.pB[:, ec * 512:(ec + 1) * 512]
                last = None
                for kc in range(KC):
                    last = T.matmul(pv, self.qp_s[:, kc * PD + ec * 128: kc * PD + ec * 128 + 128],
                                    self.B[:, kc * 512:(kc + 1) * 512],
                                    start=(kc == 0), stop=(kc == KC - 1))
                self.st.inc(last)
            self.wt(S)
            for ec in range(2):
                self.ss.inc(S.activation(qprime[:, ec * 512:(ec + 1) * 512],
                                         self.pB[:, ec * 512:(ec + 1) * 512], AF.Exp))
            self.ws(T)
            last = None
            for tt in range(TPC):
                pv = self.pB[:, 1024 + tt * 8: 1024 + tt * 8 + 1]
                for ec in range(2):
                    last = T.matmul(pv, qprime[:, ec * 512 + tt * 128: ec * 512 + (tt + 1) * 128],
                                    self.ones[:], start=(ec == 0), stop=(ec == 1))
            self.st.inc(last)
            self.wt(V)
            for tt in range(TPC):
                zc = self.zcol[:, tt:tt + 1]
                self.sv.inc(V.tensor_scalar_add(zc, self.pB[:, 1024 + tt * 8: 1024 + tt * 8 + 1], Z_EPS))
            self.wv(V)
            for tt in range(TPC):
                zc = self.zcol[:, tt:tt + 1]
                self.sv.inc(V.reciprocal(zc, zc))
            for tt in range(TPC):
                for dh in range(2):
                    i = tt * 2 + dh
                    base = 1536 if i % 2 == 0 else 1024
                    pv = self.pB[:, base: base + 512]
                    self.wv(T)
                    for ec in range(2):
                        last = T.matmul(pv, qprime[:, ec * 512 + tt * 128: ec * 512 + (tt + 1) * 128],
                                        (kv_s0 if ec == 0 else kv_s1)[:, dh * 512:(dh + 1) * 512],
                                        start=(ec == 0), stop=(ec == 1))
                    self.st.inc(last)
                    self.wt(V)
                    g = c * TPC + tt
                    dst = self.x_res[:, g * D + dh * 512: g * D + dh * 512 + 512]
                    self.sv.inc(V.scalar_tensor_tensor(dst, pv, self.zcol[:, tt:tt + 1],
                                                       dst, ALU.mult, ALU.add))
        self.wb_free_at = [self.st.cnt, self.st.cnt]

    def emit_store(self):
        nc = self.nc
        nc.sync.wait_ge(self.sv.h, self.sv.cnt)
        qtiles = self.n_tiles // 4
        for q in range(4):
            rows = self.t["o"][q * qtiles * 128:(q + 1) * qtiles * 128, :]
            dst = rows.rearrange("(t p) f -> p t f", p=128)
            src = self.x_res[:, q * qtiles * D:(q + 1) * qtiles * D]
            src = src.rearrange("p (t f) -> p t f", f=D)
            self.sout.inc(nc.sync.dma_start(out=dst, in_=src), 16)
        nc.sync.wait_ge(self.sout.h, self.sout.cnt)

    def emit_all(self):
        self.emit_loads()
        for c in range(self.n_chunks):
            self.emit_ffn_chunk(c)
        self.emit_attn()
        for c in range(self.n_chunks):
            self.emit_ffn_chunk(c)
        self.emit_store()


def build_program(n_tok=4096):
    nc = bass.Bass("TRN2", target_bir_lowering=False, debug=False)
    b = Builder(nc, n_tok)
    b.emit_all()
    return nc


def prep_inputs(L0_attn_q_proj, L0_attn_k_proj, L0_ffn_P_w, L0_ffn_U1, L0_ffn_U2,
                L0_ffn_U3, L0_ffn_W1, L0_ffn_W2, L0_ffn_b2):
    c = np.ascontiguousarray
    f = np.float32
    return {
        "w1t": c(np.asarray(L0_ffn_W1, dtype=f).transpose(0, 2, 1)),
        "w2t": c(np.asarray(L0_ffn_W2, dtype=f).transpose(0, 2, 1)),
        "pwt": c(np.asarray(L0_ffn_P_w, dtype=f).T),
        "ut": c(np.concatenate([np.asarray(L0_ffn_U1, dtype=f).T,
                                np.asarray(L0_ffn_U2, dtype=f).T,
                                np.asarray(L0_ffn_U3, dtype=f).T], axis=1)),
        "b2": c(np.asarray(L0_ffn_b2, dtype=f)),
        "qp": c(np.asarray(L0_attn_q_proj, dtype=f)),
        "kp": c(np.asarray(L0_attn_k_proj, dtype=f)),
        "ident": np.eye(128, dtype=f),
    }


def kernel(L0_attn_x, L0_attn_q_proj, L0_attn_k_proj, L0_ffn_P_w, L0_ffn_U1,
           L0_ffn_U2, L0_ffn_U3, L0_ffn_W1, L0_ffn_W2, L0_ffn_b2):
    x = np.ascontiguousarray(np.asarray(L0_attn_x, dtype=np.float32))
    B = x.shape[0]
    shared = prep_inputs(L0_attn_q_proj, L0_attn_k_proj, L0_ffn_P_w, L0_ffn_U1,
                         L0_ffn_U2, L0_ffn_U3, L0_ffn_W1, L0_ffn_W2, L0_ffn_b2)
    nc = build_program(n_tok=x.shape[1])
    in_maps = [dict(shared, x=np.ascontiguousarray(x[c])) for c in range(B)]
    res = run_bass_kernel_spmd(nc, in_maps, core_ids=list(range(B)))
    return np.stack([res.results[c]["o"] for c in range(B)], axis=0)


# version 02
# speedup vs baseline: 1.2247x; 1.2247x over previous
import numpy as np
import concourse.bass as bass
import concourse.mybir as mybir
from concourse.bass_utils import run_bass_kernel_spmd

F32 = mybir.dt.float32
BF16 = mybir.dt.bfloat16
AF = mybir.ActivationFunctionType
ALU = mybir.AluOpType
AX = mybir.AxisListType

D = 1024
KC = D // 128
NE = 12
DA = 64
PD = 256
TPC = 4
BIG = 1.0e30
INV_TAU = 1.0 / (1.0 + 1e-8)
LN_EPS = 1e-5
Z_EPS = 1e-8


class Sem:
    def __init__(self, nc, name):
        self.h = nc.semaphore(name).__enter__()
        self.cnt = 0

    def inc(self, instr, n=1):
        instr.then_inc(self.h, n)
        self.cnt += n
        return instr


class Builder:

    def __init__(self, nc, n_tok):
        self.nc = nc
        self.n_tok = n_tok
        self.n_tiles = n_tok // 128
        self.n_chunks = self.n_tiles // TPC
        t = {}
        self.t = t
        t["x"] = nc.dram_tensor("x", [n_tok, D], F32, kind="ExternalInput")
        t["o"] = nc.dram_tensor("o", [n_tok, D], F32, kind="ExternalOutput")
        t["w1t"] = nc.dram_tensor("w1t", [NE, D, D], F32, kind="ExternalInput")
        t["w2t"] = nc.dram_tensor("w2t", [NE, D, D], F32, kind="ExternalInput")
        t["pwt"] = nc.dram_tensor("pwt", [D, DA], F32, kind="ExternalInput")
        t["ut"] = nc.dram_tensor("ut", [DA, 3 * NE], F32, kind="ExternalInput")
        t["b2"] = nc.dram_tensor("b2", [NE, D], F32, kind="ExternalInput")
        t["qp"] = nc.dram_tensor("qp", [D, PD], F32, kind="ExternalInput")
        t["kp"] = nc.dram_tensor("kp", [D, PD], F32, kind="ExternalInput")
        t["ident"] = nc.dram_tensor("ident", [128, 128], F32, kind="ExternalInput")
        t["wc1"] = nc.dram_tensor("wc1", [NE, 2, 2, 128, 4096], BF16, kind="Internal")
        t["wc2"] = nc.dram_tensor("wc2", [NE, 2, 2, 128, 4096], BF16, kind="Internal")
        sb = nc.alloc_sbuf_tensor
        self.x_res = sb("x_res", [128, self.n_tiles * D], F32)
        self.A = sb("A", [128, TPC * D], F32)
        self.Bhi = sb("Bhi", [128, KC * 512], BF16)
        self.Blo = sb("Blo", [128, KC * 512], BF16)
        self.wb = [sb("wb0", [128, 2048], F32), sb("wb1", [128, 2048], F32)]
        self.ident_s = sb("ident_s", [128, 128], F32)
        self.pwhi = sb("pwhi", [128, KC * DA], BF16)
        self.pwlo = sb("pwlo", [128, KC * DA], BF16)
        self.ut_s = sb("ut_s", [DA, 3 * NE], F32)
        self.b2_s = sb("b2_s", [NE, D], F32)
        self.qphi = sb("qphi", [128, KC * PD], BF16)
        self.qplo = sb("qplo", [128, KC * PD], BF16)
        self.kphi = sb("kphi", [128, KC * PD], BF16)
        self.kplo = sb("kplo", [128, KC * PD], BF16)
        self.C1 = sb("C1", [128, TPC * NE], F32)
        self.C2 = sb("C2", [128, TPC * NE], F32)
        self.C3 = sb("C3", [128, TPC * NE], F32)
        self.C3T = sb("C3T", [NE, TPC * 128], F32)
        self.xa_s = sb("xa_s", [DA, TPC * 128], F32)
        self.z_s = sb("z_s", [128, TPC * 3 * NE], F32)
        self.mcol = sb("mcol", [128, self.n_tiles], F32)
        self.rcol = sb("rcol", [128, self.n_tiles], F32)
        self.zcol = sb("zcol", [128, TPC], F32)
        self.scr = sb("scr", [128, 16], F32)
        self.t12 = sb("t12", [128, 2 * NE], F32)
        self.ones = sb("ones", [128, 1], F32)
        self.pA = nc.alloc_psum_tensor("pA", [128, 2048], F32)
        self.pB = nc.alloc_psum_tensor("pB", [128, 2048], F32)
        self.sv = Sem(nc, "sv")
        self.ss = Sem(nc, "ss")
        self.st = Sem(nc, "st_t")
        self.sw = [Sem(nc, "sw0"), Sem(nc, "sw1")]
        self.sin = Sem(nc, "sin")
        self.swc = Sem(nc, "swc")
        self.sout = Sem(nc, "sout")
        self.wb_free_at = [0, 0]

    def wv(self, eng, val=None):
        eng.wait_ge(self.sv.h, self.sv.cnt if val is None else val)

    def ws(self, eng, val=None):
        eng.wait_ge(self.ss.h, self.ss.cnt if val is None else val)

    def wt(self, eng, val=None):
        eng.wait_ge(self.st.h, self.st.cnt if val is None else val)

    def emit_loads(self):
        nc, t = self.nc, self.t
        qtiles = self.n_tiles // 4
        for q in range(4):
            rows = t["x"][q * qtiles * 128:(q + 1) * qtiles * 128, :]
            src = rows.rearrange("(t p) f -> p t f", p=128)
            dst = self.x_res[:, q * qtiles * D:(q + 1) * qtiles * D]
            dst = dst.rearrange("p (t f) -> p t f", f=D)
            self.sin.inc(nc.sync.dma_start(out=dst, in_=src), 16)
        self.sin.inc(nc.sync.dma_start(out=self.ident_s[:], in_=t["ident"][:]), 16)
        self.sin.inc(nc.sync.dma_start(out=self.ut_s[:], in_=t["ut"][:]), 16)
        self.sin.inc(nc.sync.dma_start(out=self.b2_s[:], in_=t["b2"][:]), 16)
        for eng in (nc.vector, nc.scalar, nc.tensor):
            eng.wait_ge(self.sin.h, self.sin.cnt)
        self.sv.inc(nc.vector.memset(self.ones[:], 1.0))

    def emit_prologue(self):
        nc, t = self.nc, self.t
        V, S = nc.vector, nc.scalar
        for name, hi, lo, cols, j in (("qp", self.qphi, self.qplo, KC * PD, PD),
                                      ("kp", self.kphi, self.kplo, KC * PD, PD),
                                      ("pwt", self.pwhi, self.pwlo, KC * DA, DA)):
            stage = self.wb[0][:, 0:cols]
            src = t[name][:].rearrange("(kc p) j -> p kc j", p=128)
            nc.sync.wait_ge(self.sv.h, self.sv.cnt)
            nc.sync.wait_ge(self.ss.h, self.ss.cnt)
            self.sw[0].inc(nc.sync.dma_start(
                out=stage.rearrange("p (kc j) -> p kc j", j=j), in_=src), 16)
            S.wait_ge(self.sw[0].h, self.sw[0].cnt)
            self.ss.inc(S.copy(hi[:], stage))
            V.wait_ge(self.sw[0].h, self.sw[0].cnt)
            self.ws(V)
            self.sv.inc(V.tensor_sub(lo[:], stage, hi[:]))
        nc.sync.wait_ge(self.ss.h, self.ss.cnt)
        nc.sync.wait_ge(self.sv.h, self.sv.cnt)
        Ab = self.A[:].bitcast(BF16)
        done_at = {}
        i = 0
        for wsrc, wdst in ((t["w1t"], t["wc1"]), (t["w2t"], t["wc2"])):
            for e in range(NE):
                for oh in range(2):
                    for kb in range(2):
                        p = i % 2
                        hi_st = Ab[:, p * 4096: p * 4096 + 2048]
                        lo_st = Ab[:, p * 4096 + 2048:(p + 1) * 4096]
                        if i >= 2:
                            sA, vA = done_at[i - 2]
                            nc.sync.wait_ge(self.ss.h, sA)
                            nc.sync.wait_ge(self.sv.h, vA)
                        src = wsrc[e, kb * 512:(kb + 1) * 512, oh * 512:(oh + 1) * 512]
                        src = src.rearrange("(k p) o -> p k o", p=128)
                        dst = self.wb[p][:].rearrange("p (k o) -> p k o", o=512)
                        self.sw[p].inc(nc.sync.dma_start(out=dst, in_=src), 16)
                        S.wait_ge(self.sw[p].h, self.sw[p].cnt)
                        if i >= 2:
                            S.wait_ge(self.swc.h, 32 * i)
                        self.ss.inc(S.copy(hi_st, self.wb[p][:]))
                        V.wait_ge(self.sw[p].h, self.sw[p].cnt)
                        self.ws(V)
                        self.sv.inc(V.tensor_sub(lo_st, self.wb[p][:], hi_st))
                        done_at[i] = (self.ss.cnt, self.sv.cnt)
                        nc.sync.wait_ge(self.ss.h, self.ss.cnt)
                        nc.sync.wait_ge(self.sv.h, self.sv.cnt)
                        self.swc.inc(nc.sync.dma_start(
                            out=wdst[e, oh, kb][:, 0:2048], in_=hi_st), 16)
                        self.swc.inc(nc.sync.dma_start(
                            out=wdst[e, oh, kb][:, 2048:4096], in_=lo_st), 16)
                        i += 1
        nc.sync.wait_ge(self.swc.h, self.swc.cnt)
        S.wait_ge(self.swc.h, self.swc.cnt)
        V.wait_ge(self.swc.h, self.swc.cnt)

    def emit_ln(self, chunk, save_stats=False, use_saved=False):
        nc = self.nc
        V, S = nc.vector, nc.scalar
        for tt in range(TPC):
            g = chunk * TPC + tt
            xt = self.x_res[:, g * D:(g + 1) * D]
            At = self.A[:, tt * D:(tt + 1) * D]
            if use_saved:
                m = self.mcol[:, g:g + 1]
                r = self.rcol[:, g:g + 1]
                if tt == 0:
                    self.wt(V)
                    self.ws(V)
                    self.wv(V)
                self.sv.inc(V.tensor_scalar(At, xt, m, r, ALU.subtract, ALU.mult))
                continue
            m = self.mcol[:, g:g + 1] if save_stats else self.scr[:, 0:1]
            r = self.rcol[:, g:g + 1] if save_stats else self.scr[:, 1:2]
            sumc = self.scr[:, 2:3]
            sqc = self.scr[:, 3:4]
            varc = self.scr[:, 4:5]
            msq = self.scr[:, 5:6]
            if tt == 0:
                self.wv(V)
            self.sv.inc(V.reduce_sum(sumc, xt, axis=AX.X))
            if tt == 0:
                self.wt(S)
            self.wv(S)
            self.ss.inc(S.activation(At, xt, AF.Square, accum_out=sqc))
            self.ws(V)
            self.wv(V)
            self.sv.inc(V.tensor_scalar_mul(m, sumc, 1.0 / D))
            self.wv(V)
            self.sv.inc(V.tensor_mul(msq, m, m))
            self.sv.inc(V.tensor_scalar(varc, sqc, 1.0 / D, LN_EPS, ALU.mult, ALU.add))
            self.wv(V)
            self.sv.inc(V.tensor_sub(varc, varc, msq))
            self.wv(S)
            self.ss.inc(S.activation(varc, varc, AF.Ln))
            self.ws(S)
            self.ss.inc(S.activation(r, varc, AF.Exp, scale=-0.5))
            self.ws(V)
            self.sv.inc(V.tensor_scalar(At, xt, m, r, ALU.subtract, ALU.mult))

    def emit_transpose_A_to_B(self, arena, s0, s1):
        nc = self.nc
        T, S, V = nc.tensor, nc.scalar, nc.vector
        slot_free = [None, None]
        first = True
        for tt in range(TPC):
            for kc in range(KC):
                i = tt * KC + kc
                sl = i % 2
                base = s0 if sl == 0 else s1
                pview = arena[:, base: base + 128]
                src = self.A[:, tt * D + kc * 128: tt * D + (kc + 1) * 128]
                if first:
                    self.wv(T)
                    self.ws(T)
                    first = False
                elif slot_free[sl] is not None:
                    self.ws(T, slot_free[sl][0])
                    self.wv(T, slot_free[sl][1])
                self.st.inc(T.transpose(pview, src, self.ident_s[:]))
                self.wt(S)
                hi = self.Bhi[:, kc * 512 + tt * 128: kc * 512 + (tt + 1) * 128]
                lo = self.Blo[:, kc * 512 + tt * 128: kc * 512 + (tt + 1) * 128]
                self.ss.inc(S.copy(hi, pview))
                self.wt(V)
                self.ws(V)
                self.sv.inc(V.tensor_sub(lo, pview, hi))
                slot_free[sl] = (self.ss.cnt, self.sv.cnt)

    def _mm3(self, T, out, lhi, llo, rhi, rlo, start, stop):
        T.matmul(out, lhi, rhi, start=start, stop=False)
        T.matmul(out, lhi, rlo, start=False, stop=False)
        return T.matmul(out, llo, rhi, start=False, stop=stop)

    def emit_routing(self, chunk):
        nc = self.nc
        T, V, S = nc.tensor, nc.vector, nc.scalar
        xap = self.pB[0:DA, 0:512]
        self.ws(T)
        self.wv(T)
        ins = None
        for kc in range(KC):
            ins = self._mm3(T, xap,
                            self.pwhi[:, kc * DA:(kc + 1) * DA],
                            self.pwlo[:, kc * DA:(kc + 1) * DA],
                            self.Bhi[:, kc * 512:(kc + 1) * 512],
                            self.Blo[:, kc * 512:(kc + 1) * 512],
                            start=(kc == 0), stop=(kc == KC - 1))
        self.st.inc(ins)
        self.wt(V)
        self.sv.inc(V.tensor_copy(self.xa_s[:], xap))
        zp = self.pB[:, 512:512 + TPC * 3 * NE]
        self.wv(T)
        for tt in range(TPC):
            for r in range(3):
                self.st.inc(T.matmul(zp[:, tt * 36 + r * NE: tt * 36 + (r + 1) * NE],
                                     self.xa_s[:, tt * 128:(tt + 1) * 128],
                                     self.ut_s[:, r * NE:(r + 1) * NE],
                                     start=True, stop=True))
        self.wt(V)
        self.sv.inc(V.tensor_copy(self.z_s[:], zp))
        for tt in range(TPC):
            for r in range(3):
                z = self.z_s[:, tt * 36 + r * NE: tt * 36 + (r + 1) * NE]
                C = (self.C1, self.C2, self.C3)[r]
                Ccols = C[:, tt * NE:(tt + 1) * NE]
                m1 = self.scr[:, 6:7]
                m3 = self.scr[:, 7:8]
                bia = self.scr[:, 8:9]
                den = self.scr[:, 9:10]
                e1 = self.t12[:, 0:NE]
                zt = self.t12[:, NE:2 * NE]
                self.wv(V)
                self.sv.inc(V.reduce_max(m1, z, axis=AX.X))
                self.wv(V)
                self.sv.inc(V.tensor_scalar(e1, z, m1, None, ALU.is_ge))
                self.wv(V)
                self.sv.inc(V.scalar_tensor_tensor(zt, e1, -BIG, z, ALU.mult, ALU.add))
                self.wv(V)
                self.sv.inc(V.reduce_max(m3, zt, axis=AX.X))
                self.wv(V)
                self.sv.inc(V.tensor_scalar(e1, zt, m3, None, ALU.is_ge))
                self.wv(V)
                self.sv.inc(V.scalar_tensor_tensor(zt, e1, -BIG, zt, ALU.mult, ALU.add))
                self.wv(V)
                self.sv.inc(V.reduce_max(m3, zt, axis=AX.X))
                self.sv.inc(V.tensor_scalar_mul(bia, m1, -INV_TAU))
                self.wv(S)
                self.ss.inc(S.activation(zt, z, AF.Exp, bias=bia, scale=INV_TAU))
                self.ws(V)
                self.wv(V)
                self.sv.inc(V.tensor_scalar(e1, z, m3, None, ALU.is_ge))
                self.wv(V)
                self.sv.inc(V.tensor_mul(zt, zt, e1))
                self.wv(V)
                self.sv.inc(V.reduce_sum(den, zt, axis=AX.X))
                self.wv(V)
                self.sv.inc(V.reciprocal(den, den))
                self.wv(V)
                if r == 0:
                    self.sv.inc(V.tensor_scalar(Ccols, zt, den, None, ALU.mult))
                else:
                    self.sv.inc(V.tensor_scalar(Ccols, zt, den, 0.5, ALU.mult, ALU.mult))
        for tt in range(TPC):
            b = 1024 if tt % 2 == 0 else 1536
            pv = self.pB[0:NE, b: b + 128]
            self.wv(T)
            self.ws(T)
            self.st.inc(T.transpose(pv, self.C3[:, tt * NE:(tt + 1) * NE], self.ident_s[:]))
            self.wt(S)
            self.ss.inc(S.copy(self.C3T[:, tt * 128:(tt + 1) * 128], pv))

    def emit_mixture(self, chunk, wcache, C, dest_is_xres, first_clears):
        nc = self.nc
        T, V = nc.tensor, nc.vector
        NSU = 4 * NE

        def load_su(u):
            e, oh, kb = u // 4, (u // 2) % 2, u % 2
            p = u % 2
            src = wcache[e, oh, kb]
            dst = self.wb[p][:].bitcast(BF16)
            nc.sync.wait_ge(self.st.h, self.wb_free_at[p])
            self.sw[p].inc(nc.sync.dma_start(out=dst, in_=src), 16)

        self.wv(T)
        self.ws(T)
        sw_base = [self.sw[0].cnt, self.sw[1].cnt]
        load_su(0)
        load_su(1)
        evac_after = {}
        for u in range(NSU):
            e, oh, kb = u // 4, (u // 2) % 2, u % 2
            p = u % 2
            pair = u // 2
            grp = self.pA if pair % 2 == 0 else self.pB
            wbB = self.wb[p][:].bitcast(BF16)
            T.wait_ge(self.sw[p].h, sw_base[p] + 16 * (u // 2 + 1))
            if kb == 0 and pair >= 2:
                T.wait_ge(self.sv.h, evac_after[pair - 2])
            last = None
            for k4 in range(4):
                kc = kb * 4 + k4
                whi = wbB[:, k4 * 512:(k4 + 1) * 512]
                wlo = wbB[:, 2048 + k4 * 512: 2048 + (k4 + 1) * 512]
                for tt in range(TPC):
                    bhi = self.Bhi[:, kc * 512 + tt * 128: kc * 512 + (tt + 1) * 128]
                    blo = self.Blo[:, kc * 512 + tt * 128: kc * 512 + (tt + 1) * 128]
                    last = self._mm3(T, grp[:, tt * 512:(tt + 1) * 512],
                                     bhi, blo, whi, wlo,
                                     start=(kc == 0), stop=(kc == KC - 1))
            self.st.inc(last)
            self.wb_free_at[p] = self.st.cnt
            if u + 2 < NSU:
                load_su(u + 2)
            if kb == 1:
                self.wt(V)
                lastv = None
                for tt in range(TPC):
                    ps = grp[:, tt * 512:(tt + 1) * 512]
                    if dest_is_xres:
                        g = chunk * TPC + tt
                        dst = self.x_res[:, g * D + oh * 512: g * D + oh * 512 + 512]
                    else:
                        dst = self.A[:, tt * D + oh * 512: tt * D + oh * 512 + 512]
                    col = C[:, tt * NE + e: tt * NE + e + 1]
                    if first_clears and e == 0:
                        lastv = V.tensor_scalar(dst, ps, col, None, ALU.mult)
                    else:
                        lastv = V.scalar_tensor_tensor(dst, ps, col, dst, ALU.mult, ALU.add)
                self.sv.inc(lastv)
                evac_after[pair] = self.sv.cnt

    def emit_gelu(self):
        nc = self.nc
        self.wv(nc.scalar)
        self.ss.inc(nc.scalar.activation(self.A[:], self.A[:], AF.Gelu))

    def emit_b2_term(self, chunk):
        nc = self.nc
        T, V = nc.tensor, nc.vector
        self.wv(T)
        self.ws(T)
        for tt in range(TPC):
            for dh in range(2):
                pv = self.pA[:, ((tt * 2 + dh) % 4) * 512: ((tt * 2 + dh) % 4) * 512 + 512]
                if tt * 2 + dh >= 4:
                    self.wv(T)
                self.st.inc(T.matmul(pv, self.C3T[:, tt * 128:(tt + 1) * 128],
                                     self.b2_s[:, dh * 512:(dh + 1) * 512],
                                     start=True, stop=True))
                self.wt(V)
                g = chunk * TPC + tt
                dst = self.x_res[:, g * D + dh * 512: g * D + dh * 512 + 512]
                self.sv.inc(V.tensor_add(dst, dst, pv))

    def emit_ffn_chunk(self, chunk):
        self.emit_ln(chunk)
        self.emit_transpose_A_to_B(self.pA, 0, 512)
        self.emit_routing(chunk)
        self.emit_mixture(chunk, self.t["wc1"][:], self.C1,
                          dest_is_xres=False, first_clears=True)
        self.emit_gelu()
        self.emit_transpose_A_to_B(self.pA, 0, 512)
        self.emit_b2_term(chunk)
        self.emit_mixture(chunk, self.t["wc2"][:], self.C2,
                          dest_is_xres=True, first_clears=False)

    def emit_attn(self):
        nc = self.nc
        T, V, S = nc.tensor, nc.vector, nc.scalar
        kprime = self.wb[0][:, 0:TPC * PD]
        kv_s0 = self.wb[0][:, 1024:2048]
        kv_s1 = self.wb[1][:, 1024:2048]
        qprime = self.wb[1][:, 0:1024]
        for c in range(self.n_chunks):
            self.emit_ln(c, save_stats=True)
            self.emit_transpose_A_to_B(self.pB, 1024, 1536)
            self.ws(T)
            self.wv(T)
            if c == 0:
                self.wt(S, self.st.cnt)
            for tt in range(TPC):
                pv = self.pB[:, tt * PD:(tt + 1) * PD]
                last = None
                for kc in range(KC):
                    last = self._mm3(
                        T, pv,
                        self.Bhi[:, kc * 512 + tt * 128: kc * 512 + (tt + 1) * 128],
                        self.Blo[:, kc * 512 + tt * 128: kc * 512 + (tt + 1) * 128],
                        self.kphi[:, kc * PD:(kc + 1) * PD],
                        self.kplo[:, kc * PD:(kc + 1) * PD],
                        start=(kc == 0), stop=(kc == KC - 1))
                self.st.inc(last)
            self.wt(S)
            for tt in range(TPC):
                self.ss.inc(S.activation(kprime[:, tt * PD:(tt + 1) * PD],
                                         self.pB[:, tt * PD:(tt + 1) * PD], AF.Exp))
            self.ws(T)
            last = None
            for ec in range(2):
                for dh in range(2):
                    pv = self.pA[:, (ec * 2 + dh) * 512:(ec * 2 + dh + 1) * 512]
                    for tt in range(TPC):
                        last = T.matmul(pv,
                                        kprime[:, tt * PD + ec * 128: tt * PD + ec * 128 + 128],
                                        self.A[:, tt * D + dh * 512: tt * D + dh * 512 + 512],
                                        start=(c == 0 and tt == 0),
                                        stop=(c == self.n_chunks - 1 and tt == TPC - 1))
            self.st.inc(last)
        self.wt(V)
        self.sv.inc(V.tensor_copy(kv_s0, self.pA[:, 0:1024]))
        self.sv.inc(V.tensor_copy(kv_s1, self.pA[:, 1024:2048]))
        for c in range(self.n_chunks):
            self.emit_ln(c, use_saved=True)
            self.emit_transpose_A_to_B(self.pA, 0, 512)
            self.ws(T)
            self.wv(T)
            for ec in range(2):
                pv = self.pB[:, ec * 512:(ec + 1) * 512]
                last = None
                for kc in range(KC):
                    last = self._mm3(
                        T, pv,
                        self.qphi[:, kc * PD + ec * 128: kc * PD + ec * 128 + 128],
                        self.qplo[:, kc * PD + ec * 128: kc * PD + ec * 128 + 128],
                        self.Bhi[:, kc * 512:(kc + 1) * 512],
                        self.Blo[:, kc * 512:(kc + 1) * 512],
                        start=(kc == 0), stop=(kc == KC - 1))
                self.st.inc(last)
            self.wt(S)
            for ec in range(2):
                self.ss.inc(S.activation(qprime[:, ec * 512:(ec + 1) * 512],
                                         self.pB[:, ec * 512:(ec + 1) * 512], AF.Exp))
            self.ws(T)
            last = None
            for tt in range(TPC):
                pv = self.pB[:, 1024 + tt * 8: 1024 + tt * 8 + 1]
                for ec in range(2):
                    last = T.matmul(pv, qprime[:, ec * 512 + tt * 128: ec * 512 + (tt + 1) * 128],
                                    self.ones[:], start=(ec == 0), stop=(ec == 1))
            self.st.inc(last)
            self.wt(V)
            for tt in range(TPC):
                zc = self.zcol[:, tt:tt + 1]
                self.sv.inc(V.tensor_scalar_add(zc, self.pB[:, 1024 + tt * 8: 1024 + tt * 8 + 1], Z_EPS))
            self.wv(V)
            for tt in range(TPC):
                zc = self.zcol[:, tt:tt + 1]
                self.sv.inc(V.reciprocal(zc, zc))
            for tt in range(TPC):
                for dh in range(2):
                    i = tt * 2 + dh
                    base = 1536 if i % 2 == 0 else 1024
                    pv = self.pB[:, base: base + 512]
                    self.wv(T)
                    for ec in range(2):
                        last = T.matmul(pv, qprime[:, ec * 512 + tt * 128: ec * 512 + (tt + 1) * 128],
                                        (kv_s0 if ec == 0 else kv_s1)[:, dh * 512:(dh + 1) * 512],
                                        start=(ec == 0), stop=(ec == 1))
                    self.st.inc(last)
                    self.wt(V)
                    g = c * TPC + tt
                    dst = self.x_res[:, g * D + dh * 512: g * D + dh * 512 + 512]
                    self.sv.inc(V.scalar_tensor_tensor(dst, pv, self.zcol[:, tt:tt + 1],
                                                       dst, ALU.mult, ALU.add))
        self.wb_free_at = [self.st.cnt, self.st.cnt]

    def emit_store(self):
        nc = self.nc
        nc.sync.wait_ge(self.sv.h, self.sv.cnt)
        qtiles = self.n_tiles // 4
        for q in range(4):
            rows = self.t["o"][q * qtiles * 128:(q + 1) * qtiles * 128, :]
            dst = rows.rearrange("(t p) f -> p t f", p=128)
            src = self.x_res[:, q * qtiles * D:(q + 1) * qtiles * D]
            src = src.rearrange("p (t f) -> p t f", f=D)
            self.sout.inc(nc.sync.dma_start(out=dst, in_=src), 16)
        nc.sync.wait_ge(self.sout.h, self.sout.cnt)

    def emit_all(self):
        self.emit_loads()
        self.emit_prologue()
        for c in range(self.n_chunks):
            self.emit_ffn_chunk(c)
        self.emit_attn()
        for c in range(self.n_chunks):
            self.emit_ffn_chunk(c)
        self.emit_store()


def build_program(n_tok=4096):
    nc = bass.Bass("TRN2", target_bir_lowering=False, debug=False)
    b = Builder(nc, n_tok)
    b.emit_all()
    return nc


def prep_inputs(L0_attn_q_proj, L0_attn_k_proj, L0_ffn_P_w, L0_ffn_U1, L0_ffn_U2,
                L0_ffn_U3, L0_ffn_W1, L0_ffn_W2, L0_ffn_b2):
    c = np.ascontiguousarray
    f = np.float32
    return {
        "w1t": c(np.asarray(L0_ffn_W1, dtype=f).transpose(0, 2, 1)),
        "w2t": c(np.asarray(L0_ffn_W2, dtype=f).transpose(0, 2, 1)),
        "pwt": c(np.asarray(L0_ffn_P_w, dtype=f).T),
        "ut": c(np.concatenate([np.asarray(L0_ffn_U1, dtype=f).T,
                                np.asarray(L0_ffn_U2, dtype=f).T,
                                np.asarray(L0_ffn_U3, dtype=f).T], axis=1)),
        "b2": c(np.asarray(L0_ffn_b2, dtype=f)),
        "qp": c(np.asarray(L0_attn_q_proj, dtype=f)),
        "kp": c(np.asarray(L0_attn_k_proj, dtype=f)),
        "ident": np.eye(128, dtype=f),
    }


def kernel(L0_attn_x, L0_attn_q_proj, L0_attn_k_proj, L0_ffn_P_w, L0_ffn_U1,
           L0_ffn_U2, L0_ffn_U3, L0_ffn_W1, L0_ffn_W2, L0_ffn_b2):
    x = np.ascontiguousarray(np.asarray(L0_attn_x, dtype=np.float32))
    B = x.shape[0]
    shared = prep_inputs(L0_attn_q_proj, L0_attn_k_proj, L0_ffn_P_w, L0_ffn_U1,
                         L0_ffn_U2, L0_ffn_U3, L0_ffn_W1, L0_ffn_W2, L0_ffn_b2)
    nc = build_program(n_tok=x.shape[1])
    in_maps = [dict(shared, x=np.ascontiguousarray(x[c])) for c in range(B)]
    res = run_bass_kernel_spmd(nc, in_maps, core_ids=list(range(B)))
    return np.stack([res.results[c]["o"] for c in range(B)], axis=0)


# version 03
# speedup vs baseline: 1.3013x; 1.0626x over previous
import numpy as np
import concourse.bass as bass
import concourse.mybir as mybir
from concourse.bass_utils import run_bass_kernel_spmd

F32 = mybir.dt.float32
BF16 = mybir.dt.bfloat16
AF = mybir.ActivationFunctionType
ALU = mybir.AluOpType
AX = mybir.AxisListType

D = 1024
KC = D // 128
NE = 12
DA = 64
PD = 256
TPC = 4
BIG = 1.0e30
INV_TAU = 1.0 / (1.0 + 1e-8)
LN_EPS = 1e-5
Z_EPS = 1e-8


class Sem:
    def __init__(self, nc, name):
        self.h = nc.semaphore(name).__enter__()
        self.cnt = 0

    def inc(self, instr, n=1):
        instr.then_inc(self.h, n)
        self.cnt += n
        return instr


class Builder:

    def __init__(self, nc, n_tok):
        self.nc = nc
        self.n_tok = n_tok
        self.n_tiles = n_tok // 128
        self.n_chunks = self.n_tiles // TPC
        t = {}
        self.t = t
        t["x"] = nc.dram_tensor("x", [n_tok, D], F32, kind="ExternalInput")
        t["o"] = nc.dram_tensor("o", [n_tok, D], F32, kind="ExternalOutput")
        t["w1t"] = nc.dram_tensor("w1t", [NE, D, D], F32, kind="ExternalInput")
        t["w2t"] = nc.dram_tensor("w2t", [NE, D, D], F32, kind="ExternalInput")
        t["pwt"] = nc.dram_tensor("pwt", [D, DA], F32, kind="ExternalInput")
        t["ut"] = nc.dram_tensor("ut", [DA, 3 * NE], F32, kind="ExternalInput")
        t["b2"] = nc.dram_tensor("b2", [NE, D], F32, kind="ExternalInput")
        t["qp"] = nc.dram_tensor("qp", [D, PD], F32, kind="ExternalInput")
        t["kp"] = nc.dram_tensor("kp", [D, PD], F32, kind="ExternalInput")
        t["ident"] = nc.dram_tensor("ident", [128, 128], F32, kind="ExternalInput")
        t["wc1"] = nc.dram_tensor("wc1", [NE, 2, 2, 128, 4096], BF16, kind="Internal")
        t["wc2"] = nc.dram_tensor("wc2", [NE, 2, 2, 128, 4096], BF16, kind="Internal")
        sb = nc.alloc_sbuf_tensor
        self.x_res = sb("x_res", [128, self.n_tiles * D], F32)
        self.A = sb("A", [128, TPC * D], F32)
        self.Bhi = sb("Bhi", [128, KC * 512], BF16)
        self.Blo = sb("Blo", [128, KC * 512], BF16)
        self.wb = [sb("wb0", [128, 2048], F32), sb("wb1", [128, 2048], F32)]
        self.ident_s = sb("ident_s", [128, 128], F32)
        self.pwhi = sb("pwhi", [128, KC * DA], BF16)
        self.pwlo = sb("pwlo", [128, KC * DA], BF16)
        self.ut_s = sb("ut_s", [DA, 3 * NE], F32)
        self.b2_s = sb("b2_s", [NE, D], F32)
        self.qphi = sb("qphi", [128, KC * PD], BF16)
        self.qplo = sb("qplo", [128, KC * PD], BF16)
        self.kphi = sb("kphi", [128, KC * PD], BF16)
        self.kplo = sb("kplo", [128, KC * PD], BF16)
        self.C1 = sb("C1", [128, TPC * NE], F32)
        self.C2 = sb("C2", [128, TPC * NE], F32)
        self.C3 = sb("C3", [128, TPC * NE], F32)
        self.C3T = sb("C3T", [NE, TPC * 128], F32)
        self.xa_s = sb("xa_s", [DA, TPC * 128], F32)
        self.z_s = sb("z_s", [128, TPC * 3 * NE], F32)
        self.mcol = sb("mcol", [128, self.n_tiles], F32)
        self.rcol = sb("rcol", [128, self.n_tiles], F32)
        self.zcol = sb("zcol", [128, TPC], F32)
        self.scr = sb("scr", [128, 16], F32)
        self.g12 = sb("g12", [128, 4 * NE], F32)
        self.e144 = sb("e144", [128, TPC * 3 * NE], F32)
        self.zt144 = sb("zt144", [128, TPC * 3 * NE], F32)
        self.ones = sb("ones", [128, 1], F32)
        self.pA = nc.alloc_psum_tensor("pA", [128, 2048], F32)
        self.pB = nc.alloc_psum_tensor("pB", [128, 2048], F32)
        self.sv = Sem(nc, "sv")
        self.ss = Sem(nc, "ss")
        self.st = Sem(nc, "st_t")
        self.sw = [Sem(nc, "sw0"), Sem(nc, "sw1")]
        self.sin = Sem(nc, "sin")
        self.swc = Sem(nc, "swc")
        self.sout = Sem(nc, "sout")
        self.wb_free_at = [0, 0]

    def wv(self, eng, val=None):
        eng.wait_ge(self.sv.h, self.sv.cnt if val is None else val)

    def ws(self, eng, val=None):
        eng.wait_ge(self.ss.h, self.ss.cnt if val is None else val)

    def wt(self, eng, val=None):
        eng.wait_ge(self.st.h, self.st.cnt if val is None else val)

    def emit_loads(self):
        nc, t = self.nc, self.t
        qtiles = self.n_tiles // 4
        for q in range(4):
            rows = t["x"][q * qtiles * 128:(q + 1) * qtiles * 128, :]
            src = rows.rearrange("(t p) f -> p t f", p=128)
            dst = self.x_res[:, q * qtiles * D:(q + 1) * qtiles * D]
            dst = dst.rearrange("p (t f) -> p t f", f=D)
            self.sin.inc(nc.sync.dma_start(out=dst, in_=src), 16)
        self.sin.inc(nc.sync.dma_start(out=self.ident_s[:], in_=t["ident"][:]), 16)
        self.sin.inc(nc.sync.dma_start(out=self.ut_s[:], in_=t["ut"][:]), 16)
        self.sin.inc(nc.sync.dma_start(out=self.b2_s[:], in_=t["b2"][:]), 16)
        for eng in (nc.vector, nc.scalar, nc.tensor):
            eng.wait_ge(self.sin.h, self.sin.cnt)
        self.sv.inc(nc.vector.memset(self.ones[:], 1.0))

    def emit_prologue(self):
        nc, t = self.nc, self.t
        V, S = nc.vector, nc.scalar
        for name, hi, lo, cols, j in (("qp", self.qphi, self.qplo, KC * PD, PD),
                                      ("kp", self.kphi, self.kplo, KC * PD, PD),
                                      ("pwt", self.pwhi, self.pwlo, KC * DA, DA)):
            stage = self.wb[0][:, 0:cols]
            src = t[name][:].rearrange("(kc p) j -> p kc j", p=128)
            nc.sync.wait_ge(self.sv.h, self.sv.cnt)
            nc.sync.wait_ge(self.ss.h, self.ss.cnt)
            self.sw[0].inc(nc.sync.dma_start(
                out=stage.rearrange("p (kc j) -> p kc j", j=j), in_=src), 16)
            S.wait_ge(self.sw[0].h, self.sw[0].cnt)
            self.ss.inc(S.copy(hi[:], stage))
            V.wait_ge(self.sw[0].h, self.sw[0].cnt)
            self.ws(V)
            self.sv.inc(V.tensor_sub(lo[:], stage, hi[:]))
        nc.sync.wait_ge(self.ss.h, self.ss.cnt)
        nc.sync.wait_ge(self.sv.h, self.sv.cnt)
        Ab = self.A[:].bitcast(BF16)
        done_at = {}
        i = 0
        for wsrc, wdst in ((t["w1t"], t["wc1"]), (t["w2t"], t["wc2"])):
            for e in range(NE):
                for oh in range(2):
                    for kb in range(2):
                        p = i % 2
                        hi_st = Ab[:, p * 4096: p * 4096 + 2048]
                        lo_st = Ab[:, p * 4096 + 2048:(p + 1) * 4096]
                        if i >= 2:
                            sA, vA = done_at[i - 2]
                            nc.sync.wait_ge(self.ss.h, sA)
                            nc.sync.wait_ge(self.sv.h, vA)
                        src = wsrc[e, kb * 512:(kb + 1) * 512, oh * 512:(oh + 1) * 512]
                        src = src.rearrange("(k p) o -> p k o", p=128)
                        dst = self.wb[p][:].rearrange("p (k o) -> p k o", o=512)
                        self.sw[p].inc(nc.sync.dma_start(out=dst, in_=src), 16)
                        S.wait_ge(self.sw[p].h, self.sw[p].cnt)
                        if i >= 2:
                            S.wait_ge(self.swc.h, 32 * i)
                        self.ss.inc(S.copy(hi_st, self.wb[p][:]))
                        V.wait_ge(self.sw[p].h, self.sw[p].cnt)
                        self.ws(V)
                        self.sv.inc(V.tensor_sub(lo_st, self.wb[p][:], hi_st))
                        done_at[i] = (self.ss.cnt, self.sv.cnt)
                        nc.sync.wait_ge(self.ss.h, self.ss.cnt)
                        nc.sync.wait_ge(self.sv.h, self.sv.cnt)
                        self.swc.inc(nc.sync.dma_start(
                            out=wdst[e, oh, kb][:, 0:2048], in_=hi_st), 16)
                        self.swc.inc(nc.sync.dma_start(
                            out=wdst[e, oh, kb][:, 2048:4096], in_=lo_st), 16)
                        i += 1
        nc.sync.wait_ge(self.swc.h, self.swc.cnt)
        S.wait_ge(self.swc.h, self.swc.cnt)
        V.wait_ge(self.swc.h, self.swc.cnt)

    def emit_ln(self, chunk, save_stats=False, use_saved=False):
        nc = self.nc
        V, S = nc.vector, nc.scalar
        for tt in range(TPC):
            g = chunk * TPC + tt
            xt = self.x_res[:, g * D:(g + 1) * D]
            At = self.A[:, tt * D:(tt + 1) * D]
            if use_saved:
                m = self.mcol[:, g:g + 1]
                r = self.rcol[:, g:g + 1]
                if tt == 0:
                    self.wt(V)
                    self.ws(V)
                    self.wv(V)
                self.sv.inc(V.tensor_scalar(At, xt, m, r, ALU.subtract, ALU.mult))
                continue
            m = self.mcol[:, g:g + 1] if save_stats else self.scr[:, 0:1]
            r = self.rcol[:, g:g + 1] if save_stats else self.scr[:, 1:2]
            sumc = self.scr[:, 2:3]
            sqc = self.scr[:, 3:4]
            varc = self.scr[:, 4:5]
            msq = self.scr[:, 5:6]
            if tt == 0:
                self.wv(V)
            self.sv.inc(V.reduce_sum(sumc, xt, axis=AX.X))
            if tt == 0:
                self.wt(S)
            self.wv(S)
            self.ss.inc(S.activation(At, xt, AF.Square, accum_out=sqc))
            self.ws(V)
            self.wv(V)
            self.sv.inc(V.tensor_scalar_mul(m, sumc, 1.0 / D))
            self.wv(V)
            self.sv.inc(V.tensor_mul(msq, m, m))
            self.sv.inc(V.tensor_scalar(varc, sqc, 1.0 / D, LN_EPS, ALU.mult, ALU.add))
            self.wv(V)
            self.sv.inc(V.tensor_sub(varc, varc, msq))
            self.wv(S)
            self.ss.inc(S.activation(varc, varc, AF.Ln))
            self.ws(S)
            self.ss.inc(S.activation(r, varc, AF.Exp, scale=-0.5))
            self.ws(V)
            self.sv.inc(V.tensor_scalar(At, xt, m, r, ALU.subtract, ALU.mult))

    def emit_transpose_A_to_B(self, arena, s0, s1):
        nc = self.nc
        T, S, V = nc.tensor, nc.scalar, nc.vector
        slot_free = [None, None]
        first = True
        for tt in range(TPC):
            for kc in range(KC):
                i = tt * KC + kc
                sl = i % 2
                base = s0 if sl == 0 else s1
                pview = arena[:, base: base + 128]
                src = self.A[:, tt * D + kc * 128: tt * D + (kc + 1) * 128]
                if first:
                    self.wv(T)
                    self.ws(T)
                    first = False
                elif slot_free[sl] is not None:
                    self.ws(T, slot_free[sl][0])
                    self.wv(T, slot_free[sl][1])
                self.st.inc(T.transpose(pview, src, self.ident_s[:]))
                self.wt(S)
                hi = self.Bhi[:, kc * 512 + tt * 128: kc * 512 + (tt + 1) * 128]
                lo = self.Blo[:, kc * 512 + tt * 128: kc * 512 + (tt + 1) * 128]
                self.ss.inc(S.copy(hi, pview))
                self.wt(V)
                self.ws(V)
                self.sv.inc(V.tensor_sub(lo, pview, hi))
                slot_free[sl] = (self.ss.cnt, self.sv.cnt)

    def _mm3(self, T, out, lhi, llo, rhi, rlo, start, stop):
        T.matmul(out, lhi, rhi, start=start, stop=False)
        T.matmul(out, lhi, rlo, start=False, stop=False)
        return T.matmul(out, llo, rhi, start=False, stop=stop)

    def emit_routing(self, chunk):
        nc = self.nc
        T, V, S = nc.tensor, nc.vector, nc.scalar
        xap = self.pB[0:DA, 0:512]
        self.ws(T)
        self.wv(T)
        ins = None
        for kc in range(KC):
            ins = self._mm3(T, xap,
                            self.pwhi[:, kc * DA:(kc + 1) * DA],
                            self.pwlo[:, kc * DA:(kc + 1) * DA],
                            self.Bhi[:, kc * 512:(kc + 1) * 512],
                            self.Blo[:, kc * 512:(kc + 1) * 512],
                            start=(kc == 0), stop=(kc == KC - 1))
        self.st.inc(ins)
        self.wt(V)
        self.sv.inc(V.tensor_copy(self.xa_s[:], xap))
        zp = self.pB[:, 512:512 + TPC * 3 * NE]
        self.wv(T)
        for tt in range(TPC):
            for r in range(3):
                self.st.inc(T.matmul(zp[:, tt * 36 + r * NE: tt * 36 + (r + 1) * NE],
                                     self.xa_s[:, tt * 128:(tt + 1) * 128],
                                     self.ut_s[:, r * NE:(r + 1) * NE],
                                     start=True, stop=True))
        self.wt(V)
        self.sv.inc(V.tensor_copy(self.z_s[:], zp))
        NG = TPC * 3
        z3 = self.z_s[:].rearrange("p (g w) -> p g w", w=NE)
        e3 = self.e144[:].rearrange("p (g w) -> p g w", w=NE)
        zt3 = self.zt144[:].rearrange("p (g w) -> p g w", w=NE)
        M1 = self.g12[:, 0:NG]
        M3 = self.g12[:, NG:2 * NG]
        DEN = self.g12[:, 2 * NG:3 * NG]
        RD = self.g12[:, 3 * NG:4 * NG]
        M1b = M1.unsqueeze(2).broadcast_to([128, NG, NE])
        M3b = M3.unsqueeze(2).broadcast_to([128, NG, NE])
        self.wv(V)
        self.sv.inc(V.reduce_max(M1, z3, axis=AX.X))
        self.wv(V)
        self.sv.inc(V.tensor_tensor(e3, z3, M1b, ALU.is_ge))
        self.wv(V)
        self.sv.inc(V.scalar_tensor_tensor(zt3, e3, -BIG, z3, ALU.mult, ALU.add))
        self.wv(V)
        self.sv.inc(V.reduce_max(M3, zt3, axis=AX.X))
        self.wv(V)
        self.sv.inc(V.tensor_tensor(e3, zt3, M3b, ALU.is_ge))
        self.wv(V)
        self.sv.inc(V.scalar_tensor_tensor(zt3, e3, -BIG, zt3, ALU.mult, ALU.add))
        self.wv(V)
        self.sv.inc(V.reduce_max(M3, zt3, axis=AX.X))
        self.wv(V)
        self.sv.inc(V.tensor_tensor(zt3, z3, M1b, ALU.subtract))
        self.sv.inc(V.tensor_tensor(e3, z3, M3b, ALU.is_ge))
        self.wv(S)
        self.ss.inc(S.activation(self.zt144[:], self.zt144[:], AF.Exp, scale=INV_TAU))
        self.ws(V)
        self.sv.inc(V.tensor_mul(self.zt144[:], self.zt144[:], self.e144[:]))
        self.wv(V)
        self.sv.inc(V.reduce_sum(DEN, zt3, axis=AX.X))
        self.wv(V)
        self.sv.inc(V.reciprocal(RD, DEN))
        self.wv(V)
        self.sv.inc(V.tensor_scalar_mul(DEN, RD, 0.5))
        self.wv(V)
        zt4 = self.zt144[:].rearrange("p (t r w) -> p t r w", r=3, w=NE)
        for r in range(3):
            C = (self.C1, self.C2, self.C3)[r]
            sc = RD if r == 0 else DEN
            sc_r = sc.rearrange("p (t r) -> p t r", r=3)[:, :, r]
            sc_rb = sc_r.unsqueeze(2).broadcast_to([128, TPC, NE])
            self.sv.inc(V.tensor_tensor(
                C[:].rearrange("p (t w) -> p t w", w=NE), zt4[:, :, r, :], sc_rb,
                ALU.mult))
        for tt in range(TPC):
            b = 1024 if tt % 2 == 0 else 1536
            pv = self.pB[0:NE, b: b + 128]
            self.wv(T)
            self.ws(T)
            self.st.inc(T.transpose(pv, self.C3[:, tt * NE:(tt + 1) * NE], self.ident_s[:]))
            self.wt(S)
            self.ss.inc(S.copy(self.C3T[:, tt * 128:(tt + 1) * 128], pv))

    def load_su(self, wcache, u):
        nc = self.nc
        e, oh, kb = u // 4, (u // 2) % 2, u % 2
        p = u % 2
        src = wcache[e, oh, kb]
        dst = self.wb[p][:].bitcast(BF16)
        nc.sync.wait_ge(self.st.h, self.wb_free_at[p])
        self.sw[p].inc(nc.sync.dma_start(out=dst, in_=src), 16)

    def preload_mixture(self, wcache):
        sw_base = [self.sw[0].cnt, self.sw[1].cnt]
        self.load_su(wcache, 0)
        self.load_su(wcache, 1)
        return sw_base

    def emit_mixture(self, chunk, wcache, C, dest_is_xres, first_clears,
                     sw_base=None, hooks=None):
        nc = self.nc
        T, V = nc.tensor, nc.vector
        NSU = 4 * NE

        def load_su(u):
            self.load_su(wcache, u)

        self.wv(T)
        self.ws(T)
        if sw_base is None:
            sw_base = [self.sw[0].cnt, self.sw[1].cnt]
            load_su(0)
            load_su(1)
        evac_after = {}
        for u in range(NSU):
            e, oh, kb = u // 4, (u // 2) % 2, u % 2
            p = u % 2
            pair = u // 2
            grp = self.pA if pair % 2 == 0 else self.pB
            wbB = self.wb[p][:].bitcast(BF16)
            T.wait_ge(self.sw[p].h, sw_base[p] + 16 * (u // 2 + 1))
            if kb == 0 and pair >= 2:
                T.wait_ge(self.sv.h, evac_after[pair - 2])
            last = None
            for k4 in range(4):
                kc = kb * 4 + k4
                whi = wbB[:, k4 * 512:(k4 + 1) * 512]
                wlo = wbB[:, 2048 + k4 * 512: 2048 + (k4 + 1) * 512]
                for tt in range(TPC):
                    bhi = self.Bhi[:, kc * 512 + tt * 128: kc * 512 + (tt + 1) * 128]
                    blo = self.Blo[:, kc * 512 + tt * 128: kc * 512 + (tt + 1) * 128]
                    last = self._mm3(T, grp[:, tt * 512:(tt + 1) * 512],
                                     bhi, blo, whi, wlo,
                                     start=(kc == 0), stop=(kc == KC - 1))
            self.st.inc(last)
            self.wb_free_at[p] = self.st.cnt
            if u + 2 < NSU:
                load_su(u + 2)
            if kb == 1:
                self.wt(V)
                lastv = None
                for tt in range(TPC):
                    ps = grp[:, tt * 512:(tt + 1) * 512]
                    if dest_is_xres:
                        g = chunk * TPC + tt
                        dst = self.x_res[:, g * D + oh * 512: g * D + oh * 512 + 512]
                    else:
                        dst = self.A[:, tt * D + oh * 512: tt * D + oh * 512 + 512]
                    col = C[:, tt * NE + e: tt * NE + e + 1]
                    if first_clears and e == 0:
                        lastv = V.tensor_scalar(dst, ps, col, None, ALU.mult)
                    else:
                        lastv = V.scalar_tensor_tensor(dst, ps, col, dst, ALU.mult, ALU.add)
                self.sv.inc(lastv)
                evac_after[pair] = self.sv.cnt
                if hooks and pair in hooks:
                    hooks[pair]()

    def emit_gelu(self):
        nc = self.nc
        self.wv(nc.scalar)
        self.ss.inc(nc.scalar.activation(self.A[:], self.A[:], AF.Gelu))

    def emit_b2_term(self, chunk):
        nc = self.nc
        T, V = nc.tensor, nc.vector
        self.wv(T)
        self.ws(T)
        add_done = {}
        for tt in range(TPC):
            for dh in range(2):
                i = tt * 2 + dh
                pv = self.pA[:, (i % 4) * 512: (i % 4) * 512 + 512]
                if i >= 4:
                    self.wv(T, add_done[i - 4])
                self.st.inc(T.matmul(pv, self.C3T[:, tt * 128:(tt + 1) * 128],
                                     self.b2_s[:, dh * 512:(dh + 1) * 512],
                                     start=True, stop=True))
                self.wt(V)
                g = chunk * TPC + tt
                dst = self.x_res[:, g * D + dh * 512: g * D + dh * 512 + 512]
                self.sv.inc(V.tensor_add(dst, dst, pv))
                add_done[i] = self.sv.cnt

    def emit_ffn_chunk(self, chunk, ln_done=False):
        if not ln_done:
            self.emit_ln(chunk)
        pre1 = self.preload_mixture(self.t["wc1"][:])
        self.emit_transpose_A_to_B(self.pA, 0, 512)
        self.emit_routing(chunk)
        self.emit_mixture(chunk, self.t["wc1"][:], self.C1,
                          dest_is_xres=False, first_clears=True, sw_base=pre1)
        self.emit_gelu()
        pre2 = self.preload_mixture(self.t["wc2"][:])
        self.emit_transpose_A_to_B(self.pA, 0, 512)
        self.emit_b2_term(chunk)
        hooks = None
        if chunk + 1 < self.n_chunks:
            hooks = {4: lambda: self.emit_ln(chunk + 1)}
        self.emit_mixture(chunk, self.t["wc2"][:], self.C2,
                          dest_is_xres=True, first_clears=False, sw_base=pre2,
                          hooks=hooks)

    def emit_attn(self):
        nc = self.nc
        T, V, S = nc.tensor, nc.vector, nc.scalar
        kprime = self.wb[0][:, 0:TPC * PD]
        kv_s0 = self.wb[0][:, 1024:2048]
        kv_s1 = self.wb[1][:, 1024:2048]
        qprime = self.wb[1][:, 0:1024]
        for c in range(self.n_chunks):
            self.emit_ln(c, save_stats=True)
            self.emit_transpose_A_to_B(self.pB, 1024, 1536)
            self.ws(T)
            self.wv(T)
            if c == 0:
                self.wt(S, self.st.cnt)
            for tt in range(TPC):
                pv = self.pB[:, tt * PD:(tt + 1) * PD]
                last = None
                for kc in range(KC):
                    last = self._mm3(
                        T, pv,
                        self.Bhi[:, kc * 512 + tt * 128: kc * 512 + (tt + 1) * 128],
                        self.Blo[:, kc * 512 + tt * 128: kc * 512 + (tt + 1) * 128],
                        self.kphi[:, kc * PD:(kc + 1) * PD],
                        self.kplo[:, kc * PD:(kc + 1) * PD],
                        start=(kc == 0), stop=(kc == KC - 1))
                self.st.inc(last)
            self.wt(S)
            for tt in range(TPC):
                self.ss.inc(S.activation(kprime[:, tt * PD:(tt + 1) * PD],
                                         self.pB[:, tt * PD:(tt + 1) * PD], AF.Exp))
            self.ws(T)
            last = None
            for ec in range(2):
                for dh in range(2):
                    pv = self.pA[:, (ec * 2 + dh) * 512:(ec * 2 + dh + 1) * 512]
                    for tt in range(TPC):
                        last = T.matmul(pv,
                                        kprime[:, tt * PD + ec * 128: tt * PD + ec * 128 + 128],
                                        self.A[:, tt * D + dh * 512: tt * D + dh * 512 + 512],
                                        start=(c == 0 and tt == 0),
                                        stop=(c == self.n_chunks - 1 and tt == TPC - 1))
            self.st.inc(last)
        self.wt(V)
        self.sv.inc(V.tensor_copy(kv_s0, self.pA[:, 0:1024]))
        self.sv.inc(V.tensor_copy(kv_s1, self.pA[:, 1024:2048]))
        for c in range(self.n_chunks):
            self.emit_ln(c, use_saved=True)
            self.emit_transpose_A_to_B(self.pA, 0, 512)
            self.ws(T)
            self.wv(T)
            for ec in range(2):
                pv = self.pB[:, ec * 512:(ec + 1) * 512]
                last = None
                for kc in range(KC):
                    last = self._mm3(
                        T, pv,
                        self.qphi[:, kc * PD + ec * 128: kc * PD + ec * 128 + 128],
                        self.qplo[:, kc * PD + ec * 128: kc * PD + ec * 128 + 128],
                        self.Bhi[:, kc * 512:(kc + 1) * 512],
                        self.Blo[:, kc * 512:(kc + 1) * 512],
                        start=(kc == 0), stop=(kc == KC - 1))
                self.st.inc(last)
            self.wt(S)
            for ec in range(2):
                self.ss.inc(S.activation(qprime[:, ec * 512:(ec + 1) * 512],
                                         self.pB[:, ec * 512:(ec + 1) * 512], AF.Exp))
            self.ws(T)
            last = None
            for tt in range(TPC):
                pv = self.pB[:, 1024 + tt * 8: 1024 + tt * 8 + 1]
                for ec in range(2):
                    last = T.matmul(pv, qprime[:, ec * 512 + tt * 128: ec * 512 + (tt + 1) * 128],
                                    self.ones[:], start=(ec == 0), stop=(ec == 1))
            self.st.inc(last)
            self.wt(V)
            for tt in range(TPC):
                zc = self.zcol[:, tt:tt + 1]
                self.sv.inc(V.tensor_scalar_add(zc, self.pB[:, 1024 + tt * 8: 1024 + tt * 8 + 1], Z_EPS))
            self.wv(V)
            for tt in range(TPC):
                zc = self.zcol[:, tt:tt + 1]
                self.sv.inc(V.reciprocal(zc, zc))
            num_done = {}
            for tt in range(TPC):
                for dh in range(2):
                    i = tt * 2 + dh
                    base = 1536 if i % 2 == 0 else 1024
                    pv = self.pB[:, base: base + 512]
                    if i < 2:
                        self.wv(T)
                    else:
                        self.wv(T, num_done[i - 2])
                    for ec in range(2):
                        last = T.matmul(pv, qprime[:, ec * 512 + tt * 128: ec * 512 + (tt + 1) * 128],
                                        (kv_s0 if ec == 0 else kv_s1)[:, dh * 512:(dh + 1) * 512],
                                        start=(ec == 0), stop=(ec == 1))
                    self.st.inc(last)
                    self.wt(V)
                    g = c * TPC + tt
                    dst = self.x_res[:, g * D + dh * 512: g * D + dh * 512 + 512]
                    self.sv.inc(V.scalar_tensor_tensor(dst, pv, self.zcol[:, tt:tt + 1],
                                                       dst, ALU.mult, ALU.add))
                    num_done[i] = self.sv.cnt
        self.wb_free_at = [self.st.cnt, self.st.cnt]

    def emit_store(self):
        nc = self.nc
        nc.sync.wait_ge(self.sv.h, self.sv.cnt)
        qtiles = self.n_tiles // 4
        for q in range(4):
            rows = self.t["o"][q * qtiles * 128:(q + 1) * qtiles * 128, :]
            dst = rows.rearrange("(t p) f -> p t f", p=128)
            src = self.x_res[:, q * qtiles * D:(q + 1) * qtiles * D]
            src = src.rearrange("p (t f) -> p t f", f=D)
            self.sout.inc(nc.sync.dma_start(out=dst, in_=src), 16)
        nc.sync.wait_ge(self.sout.h, self.sout.cnt)

    def emit_all(self):
        self.emit_loads()
        self.emit_prologue()
        for c in range(self.n_chunks):
            self.emit_ffn_chunk(c, ln_done=(c > 0))
        self.emit_attn()
        for c in range(self.n_chunks):
            self.emit_ffn_chunk(c, ln_done=(c > 0))
        self.emit_store()


def build_program(n_tok=4096):
    nc = bass.Bass("TRN2", target_bir_lowering=False, debug=False)
    b = Builder(nc, n_tok)
    b.emit_all()
    return nc


def prep_inputs(L0_attn_q_proj, L0_attn_k_proj, L0_ffn_P_w, L0_ffn_U1, L0_ffn_U2,
                L0_ffn_U3, L0_ffn_W1, L0_ffn_W2, L0_ffn_b2):
    c = np.ascontiguousarray
    f = np.float32
    return {
        "w1t": c(np.asarray(L0_ffn_W1, dtype=f).transpose(0, 2, 1)),
        "w2t": c(np.asarray(L0_ffn_W2, dtype=f).transpose(0, 2, 1)),
        "pwt": c(np.asarray(L0_ffn_P_w, dtype=f).T),
        "ut": c(np.concatenate([np.asarray(L0_ffn_U1, dtype=f).T,
                                np.asarray(L0_ffn_U2, dtype=f).T,
                                np.asarray(L0_ffn_U3, dtype=f).T], axis=1)),
        "b2": c(np.asarray(L0_ffn_b2, dtype=f)),
        "qp": c(np.asarray(L0_attn_q_proj, dtype=f)),
        "kp": c(np.asarray(L0_attn_k_proj, dtype=f)),
        "ident": np.eye(128, dtype=f),
    }


def kernel(L0_attn_x, L0_attn_q_proj, L0_attn_k_proj, L0_ffn_P_w, L0_ffn_U1,
           L0_ffn_U2, L0_ffn_U3, L0_ffn_W1, L0_ffn_W2, L0_ffn_b2):
    x = np.ascontiguousarray(np.asarray(L0_attn_x, dtype=np.float32))
    B = x.shape[0]
    shared = prep_inputs(L0_attn_q_proj, L0_attn_k_proj, L0_ffn_P_w, L0_ffn_U1,
                         L0_ffn_U2, L0_ffn_U3, L0_ffn_W1, L0_ffn_W2, L0_ffn_b2)
    nc = build_program(n_tok=x.shape[1])
    in_maps = [dict(shared, x=np.ascontiguousarray(x[c])) for c in range(B)]
    res = run_bass_kernel_spmd(nc, in_maps, core_ids=list(range(B)))
    return np.stack([res.results[c]["o"] for c in range(B)], axis=0)


# version 04
# speedup vs baseline: 1.4356x; 1.1032x over previous
import numpy as np
import concourse.bass as bass
import concourse.mybir as mybir
from concourse.bass_utils import run_bass_kernel_spmd

F32 = mybir.dt.float32
BF16 = mybir.dt.bfloat16
AF = mybir.ActivationFunctionType
ALU = mybir.AluOpType
AX = mybir.AxisListType

D = 1024
KC = D // 128
NE = 12
DA = 64
PD = 256
TPC = 4
BIG = 1.0e30
INV_TAU = 1.0 / (1.0 + 1e-8)
LN_EPS = 1e-5
Z_EPS = 1e-8


class Sem:
    def __init__(self, nc, name):
        self.h = nc.semaphore(name).__enter__()
        self.cnt = 0

    def inc(self, instr, n=1):
        instr.then_inc(self.h, n)
        self.cnt += n
        return instr


class Builder:

    def __init__(self, nc, n_tok):
        self.nc = nc
        self.n_tok = n_tok
        self.n_tiles = n_tok // 128
        self.n_chunks = self.n_tiles // TPC
        t = {}
        self.t = t
        t["x"] = nc.dram_tensor("x", [n_tok, D], F32, kind="ExternalInput")
        t["o"] = nc.dram_tensor("o", [n_tok, D], F32, kind="ExternalOutput")
        t["w1t"] = nc.dram_tensor("w1t", [NE, D, D], F32, kind="ExternalInput")
        t["w2t"] = nc.dram_tensor("w2t", [NE, D, D], F32, kind="ExternalInput")
        t["pwt"] = nc.dram_tensor("pwt", [D, DA], F32, kind="ExternalInput")
        t["ut"] = nc.dram_tensor("ut", [DA, 3 * NE], F32, kind="ExternalInput")
        t["b2"] = nc.dram_tensor("b2", [NE, D], F32, kind="ExternalInput")
        t["qp"] = nc.dram_tensor("qp", [D, PD], F32, kind="ExternalInput")
        t["kp"] = nc.dram_tensor("kp", [D, PD], F32, kind="ExternalInput")
        t["ident"] = nc.dram_tensor("ident", [128, 128], F32, kind="ExternalInput")
        t["wc1"] = nc.dram_tensor("wc1", [NE, 2, 2, 128, 4096], BF16, kind="Internal")
        t["wc2"] = nc.dram_tensor("wc2", [NE, 2, 2, 128, 4096], BF16, kind="Internal")
        sb = nc.alloc_sbuf_tensor
        self.x_res = sb("x_res", [128, self.n_tiles * D], F32)
        self.A = sb("A", [128, TPC * D], F32)
        self.Bhi = sb("Bhi", [128, KC * 512], BF16)
        self.Blo = sb("Blo", [128, KC * 512], BF16)
        self.wb = [sb("wb0", [128, 2048], F32), sb("wb1", [128, 2048], F32)]
        self.ident_s = sb("ident_s", [128, 128], F32)
        self.pwhi = sb("pwhi", [128, KC * DA], BF16)
        self.pwlo = sb("pwlo", [128, KC * DA], BF16)
        self.ut_s = sb("ut_s", [DA, 3 * NE], F32)
        self.b2_s = sb("b2_s", [NE, D], F32)
        self.qphi = sb("qphi", [128, KC * PD], BF16)
        self.qplo = sb("qplo", [128, KC * PD], BF16)
        self.kphi = sb("kphi", [128, KC * PD], BF16)
        self.kplo = sb("kplo", [128, KC * PD], BF16)
        self.C1 = sb("C1", [128, TPC * NE], F32)
        self.C2 = sb("C2", [128, TPC * NE], F32)
        self.C3 = sb("C3", [128, TPC * NE], F32)
        self.C3T = sb("C3T", [NE, TPC * 128], F32)
        self.xa_s = sb("xa_s", [DA, TPC * 128], F32)
        self.z_s = sb("z_s", [128, TPC * 3 * NE], F32)
        self.mcol = sb("mcol", [128, self.n_tiles], F32)
        self.rcol = sb("rcol", [128, self.n_tiles], F32)
        self.zcol = sb("zcol", [128, TPC], F32)
        self.scr = sb("scr", [128, 16], F32)
        self.g12 = sb("g12", [128, 4 * NE], F32)
        self.e144 = sb("e144", [128, TPC * 3 * NE], F32)
        self.zt144 = sb("zt144", [128, TPC * 3 * NE], F32)
        self.ones = sb("ones", [128, 1], F32)
        self.pA = nc.alloc_psum_tensor("pA", [128, 2048], F32)
        self.pB = nc.alloc_psum_tensor("pB", [128, 2048], F32)
        self.sv = Sem(nc, "sv")
        self.ss = Sem(nc, "ss")
        self.st = Sem(nc, "st_t")
        self.sw = [Sem(nc, "sw0"), Sem(nc, "sw1")]
        self.sin = Sem(nc, "sin")
        self.swc = Sem(nc, "swc")
        self.sout = Sem(nc, "sout")
        self.wb_free_at = [0, 0]

    def wv(self, eng, val=None):
        eng.wait_ge(self.sv.h, self.sv.cnt if val is None else val)

    def ws(self, eng, val=None):
        eng.wait_ge(self.ss.h, self.ss.cnt if val is None else val)

    def wt(self, eng, val=None):
        eng.wait_ge(self.st.h, self.st.cnt if val is None else val)

    def emit_loads(self):
        nc, t = self.nc, self.t
        qtiles = self.n_tiles // 4
        for q in range(4):
            rows = t["x"][q * qtiles * 128:(q + 1) * qtiles * 128, :]
            src = rows.rearrange("(t p) f -> p t f", p=128)
            dst = self.x_res[:, q * qtiles * D:(q + 1) * qtiles * D]
            dst = dst.rearrange("p (t f) -> p t f", f=D)
            self.sin.inc(nc.sync.dma_start(out=dst, in_=src), 16)
        self.sin.inc(nc.sync.dma_start(out=self.ident_s[:], in_=t["ident"][:]), 16)
        self.sin.inc(nc.sync.dma_start(out=self.ut_s[:], in_=t["ut"][:]), 16)
        self.sin.inc(nc.sync.dma_start(out=self.b2_s[:], in_=t["b2"][:]), 16)
        for eng in (nc.vector, nc.scalar, nc.tensor):
            eng.wait_ge(self.sin.h, self.sin.cnt)
        self.sv.inc(nc.vector.memset(self.ones[:], 1.0))

    def emit_prologue(self):
        nc, t = self.nc, self.t
        V, S = nc.vector, nc.scalar
        for name, hi, lo, cols, j in (("qp", self.qphi, self.qplo, KC * PD, PD),
                                      ("kp", self.kphi, self.kplo, KC * PD, PD),
                                      ("pwt", self.pwhi, self.pwlo, KC * DA, DA)):
            stage = self.wb[0][:, 0:cols]
            src = t[name][:].rearrange("(kc p) j -> p kc j", p=128)
            nc.sync.wait_ge(self.sv.h, self.sv.cnt)
            nc.sync.wait_ge(self.ss.h, self.ss.cnt)
            self.sw[0].inc(nc.sync.dma_start(
                out=stage.rearrange("p (kc j) -> p kc j", j=j), in_=src), 16)
            S.wait_ge(self.sw[0].h, self.sw[0].cnt)
            self.ss.inc(S.copy(hi[:], stage))
            V.wait_ge(self.sw[0].h, self.sw[0].cnt)
            self.ws(V)
            self.sv.inc(V.tensor_sub(lo[:], stage, hi[:]))
        nc.sync.wait_ge(self.ss.h, self.ss.cnt)
        nc.sync.wait_ge(self.sv.h, self.sv.cnt)
        Ab = self.A[:].bitcast(BF16)
        done_at = {}
        i = 0
        for wsrc, wdst in ((t["w1t"], t["wc1"]), (t["w2t"], t["wc2"])):
            for e in range(NE):
                for oh in range(2):
                    for kb in range(2):
                        p = i % 2
                        hi_st = Ab[:, p * 4096: p * 4096 + 2048]
                        lo_st = Ab[:, p * 4096 + 2048:(p + 1) * 4096]
                        if i >= 2:
                            sA, vA = done_at[i - 2]
                            nc.sync.wait_ge(self.ss.h, sA)
                            nc.sync.wait_ge(self.sv.h, vA)
                        src = wsrc[e, kb * 512:(kb + 1) * 512, oh * 512:(oh + 1) * 512]
                        src = src.rearrange("(k p) o -> p k o", p=128)
                        dst = self.wb[p][:].rearrange("p (k o) -> p k o", o=512)
                        self.sw[p].inc(nc.sync.dma_start(out=dst, in_=src), 16)
                        S.wait_ge(self.sw[p].h, self.sw[p].cnt)
                        if i >= 2:
                            S.wait_ge(self.swc.h, 32 * i)
                        self.ss.inc(S.copy(hi_st, self.wb[p][:]))
                        V.wait_ge(self.sw[p].h, self.sw[p].cnt)
                        self.ws(V)
                        self.sv.inc(V.tensor_sub(lo_st, self.wb[p][:], hi_st))
                        done_at[i] = (self.ss.cnt, self.sv.cnt)
                        nc.scalar.wait_ge(self.sv.h, self.sv.cnt)
                        self.swc.inc(nc.scalar.dma_start(
                            out=wdst[e, oh, kb][:, 0:2048], in_=hi_st), 16)
                        self.swc.inc(nc.scalar.dma_start(
                            out=wdst[e, oh, kb][:, 2048:4096], in_=lo_st), 16)
                        i += 1
        nc.sync.wait_ge(self.swc.h, self.swc.cnt)
        S.wait_ge(self.swc.h, self.swc.cnt)
        V.wait_ge(self.swc.h, self.swc.cnt)

    def emit_ln(self, chunk, save_stats=False, use_saved=False):
        nc = self.nc
        V, S = nc.vector, nc.scalar
        for tt in range(TPC):
            g = chunk * TPC + tt
            xt = self.x_res[:, g * D:(g + 1) * D]
            At = self.A[:, tt * D:(tt + 1) * D]
            if use_saved:
                m = self.mcol[:, g:g + 1]
                r = self.rcol[:, g:g + 1]
                if tt == 0:
                    self.wt(V)
                    self.ws(V)
                    self.wv(V)
                self.sv.inc(V.tensor_scalar(At, xt, m, r, ALU.subtract, ALU.mult))
                continue
            m = self.mcol[:, g:g + 1] if save_stats else self.scr[:, 0:1]
            r = self.rcol[:, g:g + 1] if save_stats else self.scr[:, 1:2]
            sumc = self.scr[:, 2:3]
            sqc = self.scr[:, 3:4]
            varc = self.scr[:, 4:5]
            msq = self.scr[:, 5:6]
            if tt == 0:
                self.wv(V)
            self.sv.inc(V.reduce_sum(sumc, xt, axis=AX.X))
            if tt == 0:
                self.wt(S)
            self.wv(S)
            self.ss.inc(S.activation(At, xt, AF.Square, accum_out=sqc))
            self.ws(V)
            self.wv(V)
            self.sv.inc(V.tensor_scalar_mul(m, sumc, 1.0 / D))
            self.wv(V)
            self.sv.inc(V.tensor_mul(msq, m, m))
            self.sv.inc(V.tensor_scalar(varc, sqc, 1.0 / D, LN_EPS, ALU.mult, ALU.add))
            self.wv(V)
            self.sv.inc(V.tensor_sub(varc, varc, msq))
            self.wv(S)
            self.ss.inc(S.activation(varc, varc, AF.Ln))
            self.ws(S)
            self.ss.inc(S.activation(r, varc, AF.Exp, scale=-0.5))
            self.ws(V)
            self.sv.inc(V.tensor_scalar(At, xt, m, r, ALU.subtract, ALU.mult))

    def emit_transpose_A_to_B(self, arena, s0, s1):
        nc = self.nc
        T, S, V = nc.tensor, nc.scalar, nc.vector
        slot_free = [None, None]
        first = True
        for tt in range(TPC):
            for kc in range(KC):
                i = tt * KC + kc
                sl = i % 2
                base = s0 if sl == 0 else s1
                pview = arena[:, base: base + 128]
                src = self.A[:, tt * D + kc * 128: tt * D + (kc + 1) * 128]
                if first:
                    self.wv(T)
                    self.ws(T)
                    first = False
                elif slot_free[sl] is not None:
                    self.ws(T, slot_free[sl][0])
                    self.wv(T, slot_free[sl][1])
                self.st.inc(T.transpose(pview, src, self.ident_s[:]))
                self.wt(S)
                hi = self.Bhi[:, kc * 512 + tt * 128: kc * 512 + (tt + 1) * 128]
                lo = self.Blo[:, kc * 512 + tt * 128: kc * 512 + (tt + 1) * 128]
                self.ss.inc(S.copy(hi, pview))
                self.wt(V)
                self.ws(V)
                self.sv.inc(V.tensor_sub(lo, pview, hi))
                slot_free[sl] = (self.ss.cnt, self.sv.cnt)

    def _mm3(self, T, out, lhi, llo, rhi, rlo, start, stop):
        T.matmul(out, lhi, rhi, start=start, stop=False)
        T.matmul(out, lhi, rlo, start=False, stop=False)
        return T.matmul(out, llo, rhi, start=False, stop=stop)

    def emit_routing(self, chunk):
        nc = self.nc
        T, V, S = nc.tensor, nc.vector, nc.scalar
        xap = self.pB[0:DA, 0:512]
        self.ws(T)
        self.wv(T)
        ins = None
        for kc in range(KC):
            ins = self._mm3(T, xap,
                            self.pwhi[:, kc * DA:(kc + 1) * DA],
                            self.pwlo[:, kc * DA:(kc + 1) * DA],
                            self.Bhi[:, kc * 512:(kc + 1) * 512],
                            self.Blo[:, kc * 512:(kc + 1) * 512],
                            start=(kc == 0), stop=(kc == KC - 1))
        self.st.inc(ins)
        self.wt(V)
        self.sv.inc(V.tensor_copy(self.xa_s[:], xap))
        zp = self.pB[:, 512:512 + TPC * 3 * NE]
        self.wv(T)
        for tt in range(TPC):
            for r in range(3):
                self.st.inc(T.matmul(zp[:, tt * 36 + r * NE: tt * 36 + (r + 1) * NE],
                                     self.xa_s[:, tt * 128:(tt + 1) * 128],
                                     self.ut_s[:, r * NE:(r + 1) * NE],
                                     start=True, stop=True))
        self.wt(V)
        self.sv.inc(V.tensor_copy(self.z_s[:], zp))
        NG = TPC * 3
        z3 = self.z_s[:].rearrange("p (g w) -> p g w", w=NE)
        e3 = self.e144[:].rearrange("p (g w) -> p g w", w=NE)
        zt3 = self.zt144[:].rearrange("p (g w) -> p g w", w=NE)
        M1 = self.g12[:, 0:NG]
        M3 = self.g12[:, NG:2 * NG]
        DEN = self.g12[:, 2 * NG:3 * NG]
        RD = self.g12[:, 3 * NG:4 * NG]
        M1b = M1.unsqueeze(2).broadcast_to([128, NG, NE])
        M3b = M3.unsqueeze(2).broadcast_to([128, NG, NE])
        self.wv(V)
        self.sv.inc(V.reduce_max(M1, z3, axis=AX.X))
        self.wv(V)
        self.sv.inc(V.tensor_tensor(e3, z3, M1b, ALU.is_ge))
        self.wv(V)
        self.sv.inc(V.scalar_tensor_tensor(zt3, e3, -BIG, z3, ALU.mult, ALU.add))
        self.wv(V)
        self.sv.inc(V.reduce_max(M3, zt3, axis=AX.X))
        self.wv(V)
        self.sv.inc(V.tensor_tensor(e3, zt3, M3b, ALU.is_ge))
        self.wv(V)
        self.sv.inc(V.scalar_tensor_tensor(zt3, e3, -BIG, zt3, ALU.mult, ALU.add))
        self.wv(V)
        self.sv.inc(V.reduce_max(M3, zt3, axis=AX.X))
        self.wv(V)
        self.sv.inc(V.tensor_tensor(zt3, z3, M1b, ALU.subtract))
        self.sv.inc(V.tensor_tensor(e3, z3, M3b, ALU.is_ge))
        self.wv(S)
        self.ss.inc(S.activation(self.zt144[:], self.zt144[:], AF.Exp, scale=INV_TAU))
        self.ws(V)
        self.sv.inc(V.tensor_mul(self.zt144[:], self.zt144[:], self.e144[:]))
        self.wv(V)
        self.sv.inc(V.reduce_sum(DEN, zt3, axis=AX.X))
        self.wv(V)
        self.sv.inc(V.reciprocal(RD, DEN))
        self.wv(V)
        self.sv.inc(V.tensor_scalar_mul(DEN, RD, 0.5))
        self.wv(V)
        zt4 = self.zt144[:].rearrange("p (t r w) -> p t r w", r=3, w=NE)
        for r in range(3):
            C = (self.C1, self.C2, self.C3)[r]
            sc = RD if r == 0 else DEN
            sc_r = sc.rearrange("p (t r) -> p t r", r=3)[:, :, r]
            sc_rb = sc_r.unsqueeze(2).broadcast_to([128, TPC, NE])
            self.sv.inc(V.tensor_tensor(
                C[:].rearrange("p (t w) -> p t w", w=NE), zt4[:, :, r, :], sc_rb,
                ALU.mult))
        for tt in range(TPC):
            b = 1024 if tt % 2 == 0 else 1536
            pv = self.pB[0:NE, b: b + 128]
            self.wv(T)
            self.ws(T)
            self.st.inc(T.transpose(pv, self.C3[:, tt * NE:(tt + 1) * NE], self.ident_s[:]))
            self.wt(S)
            self.ss.inc(S.copy(self.C3T[:, tt * 128:(tt + 1) * 128], pv))

    def load_su(self, wcache, u):
        nc = self.nc
        e, oh, kb = u // 4, (u // 2) % 2, u % 2
        p = u % 2
        src = wcache[e, oh, kb]
        dst = self.wb[p][:].bitcast(BF16)
        nc.sync.wait_ge(self.st.h, self.wb_free_at[p])
        self.sw[p].inc(nc.sync.dma_start(out=dst, in_=src), 16)

    def preload_mixture(self, wcache):
        sw_base = [self.sw[0].cnt, self.sw[1].cnt]
        self.load_su(wcache, 0)
        self.load_su(wcache, 1)
        return sw_base

    def emit_mixture(self, chunk, wcache, C, dest_is_xres, first_clears,
                     sw_base=None, hooks=None):
        nc = self.nc
        T, V = nc.tensor, nc.vector
        NSU = 4 * NE

        def load_su(u):
            self.load_su(wcache, u)

        self.wv(T)
        self.ws(T)
        if sw_base is None:
            sw_base = [self.sw[0].cnt, self.sw[1].cnt]
            load_su(0)
            load_su(1)
        evac_after = {}
        for u in range(NSU):
            e, oh, kb = u // 4, (u // 2) % 2, u % 2
            p = u % 2
            pair = u // 2
            grp = self.pA if pair % 2 == 0 else self.pB
            wbB = self.wb[p][:].bitcast(BF16)
            T.wait_ge(self.sw[p].h, sw_base[p] + 16 * (u // 2 + 1))
            if kb == 0 and pair >= 2:
                T.wait_ge(self.sv.h, evac_after[pair - 2])
            last = None
            for k4 in range(4):
                kc = kb * 4 + k4
                whi = wbB[:, k4 * 512:(k4 + 1) * 512]
                wlo = wbB[:, 2048 + k4 * 512: 2048 + (k4 + 1) * 512]
                for tt in range(TPC):
                    bhi = self.Bhi[:, kc * 512 + tt * 128: kc * 512 + (tt + 1) * 128]
                    blo = self.Blo[:, kc * 512 + tt * 128: kc * 512 + (tt + 1) * 128]
                    last = self._mm3(T, grp[:, tt * 512:(tt + 1) * 512],
                                     bhi, blo, whi, wlo,
                                     start=(kc == 0), stop=(kc == KC - 1))
            self.st.inc(last)
            self.wb_free_at[p] = self.st.cnt
            if u + 2 < NSU:
                load_su(u + 2)
            if kb == 1:
                self.wt(V)
                lastv = None
                for tt in range(TPC):
                    ps = grp[:, tt * 512:(tt + 1) * 512]
                    if dest_is_xres:
                        g = chunk * TPC + tt
                        dst = self.x_res[:, g * D + oh * 512: g * D + oh * 512 + 512]
                    else:
                        dst = self.A[:, tt * D + oh * 512: tt * D + oh * 512 + 512]
                    col = C[:, tt * NE + e: tt * NE + e + 1]
                    if first_clears and e == 0:
                        lastv = V.tensor_scalar(dst, ps, col, None, ALU.mult)
                    else:
                        lastv = V.scalar_tensor_tensor(dst, ps, col, dst, ALU.mult, ALU.add)
                self.sv.inc(lastv)
                evac_after[pair] = self.sv.cnt
                if hooks and pair in hooks:
                    hooks[pair]()

    def emit_gelu(self):
        nc = self.nc
        self.wv(nc.scalar)
        self.ss.inc(nc.scalar.activation(self.A[:], self.A[:], AF.Gelu))

    def emit_b2_term(self, chunk):
        nc = self.nc
        T, V = nc.tensor, nc.vector
        self.wv(T)
        self.ws(T)
        add_done = {}
        for tt in range(TPC):
            for dh in range(2):
                i = tt * 2 + dh
                pv = self.pA[:, (i % 4) * 512: (i % 4) * 512 + 512]
                if i >= 4:
                    self.wv(T, add_done[i - 4])
                self.st.inc(T.matmul(pv, self.C3T[:, tt * 128:(tt + 1) * 128],
                                     self.b2_s[:, dh * 512:(dh + 1) * 512],
                                     start=True, stop=True))
                self.wt(V)
                g = chunk * TPC + tt
                dst = self.x_res[:, g * D + dh * 512: g * D + dh * 512 + 512]
                self.sv.inc(V.tensor_add(dst, dst, pv))
                add_done[i] = self.sv.cnt

    def emit_ffn_chunk(self, chunk, ln_done=False):
        if not ln_done:
            self.emit_ln(chunk)
        pre1 = self.preload_mixture(self.t["wc1"][:])
        self.emit_transpose_A_to_B(self.pA, 0, 512)
        self.emit_routing(chunk)
        self.emit_mixture(chunk, self.t["wc1"][:], self.C1,
                          dest_is_xres=False, first_clears=True, sw_base=pre1)
        self.emit_gelu()
        pre2 = self.preload_mixture(self.t["wc2"][:])
        self.emit_transpose_A_to_B(self.pA, 0, 512)
        self.emit_b2_term(chunk)
        hooks = None
        if chunk + 1 < self.n_chunks:
            hooks = {4: lambda: self.emit_ln(chunk + 1)}
        self.emit_mixture(chunk, self.t["wc2"][:], self.C2,
                          dest_is_xres=True, first_clears=False, sw_base=pre2,
                          hooks=hooks)

    def emit_attn(self):
        nc = self.nc
        T, V, S = nc.tensor, nc.vector, nc.scalar
        kprime = self.wb[0][:, 0:TPC * PD]
        kv_s0 = self.wb[0][:, 1024:2048]
        kv_s1 = self.wb[1][:, 1024:2048]
        qprime = self.wb[1][:, 0:1024]
        for c in range(self.n_chunks):
            self.emit_ln(c, save_stats=True)
            self.emit_transpose_A_to_B(self.pB, 1024, 1536)
            self.ws(T)
            self.wv(T)
            if c == 0:
                self.wt(S, self.st.cnt)
            for tt in range(TPC):
                pv = self.pB[:, tt * PD:(tt + 1) * PD]
                last = None
                for kc in range(KC):
                    last = self._mm3(
                        T, pv,
                        self.Bhi[:, kc * 512 + tt * 128: kc * 512 + (tt + 1) * 128],
                        self.Blo[:, kc * 512 + tt * 128: kc * 512 + (tt + 1) * 128],
                        self.kphi[:, kc * PD:(kc + 1) * PD],
                        self.kplo[:, kc * PD:(kc + 1) * PD],
                        start=(kc == 0), stop=(kc == KC - 1))
                self.st.inc(last)
            self.wt(S)
            for tt in range(TPC):
                self.ss.inc(S.activation(kprime[:, tt * PD:(tt + 1) * PD],
                                         self.pB[:, tt * PD:(tt + 1) * PD], AF.Exp))
            self.ws(T)
            last = None
            for ec in range(2):
                for dh in range(2):
                    pv = self.pA[:, (ec * 2 + dh) * 512:(ec * 2 + dh + 1) * 512]
                    for tt in range(TPC):
                        last = T.matmul(pv,
                                        kprime[:, tt * PD + ec * 128: tt * PD + ec * 128 + 128],
                                        self.A[:, tt * D + dh * 512: tt * D + dh * 512 + 512],
                                        start=(c == 0 and tt == 0),
                                        stop=(c == self.n_chunks - 1 and tt == TPC - 1))
            self.st.inc(last)
        self.wt(V)
        self.sv.inc(V.tensor_copy(kv_s0, self.pA[:, 0:1024]))
        self.sv.inc(V.tensor_copy(kv_s1, self.pA[:, 1024:2048]))
        for c in range(self.n_chunks):
            if c == 0:
                self.emit_ln(c, use_saved=True)
            self.emit_transpose_A_to_B(self.pA, 0, 512)
            self.ws(T)
            self.wv(T)
            for ec in range(2):
                pv = self.pB[:, ec * 512:(ec + 1) * 512]
                last = None
                for kc in range(KC):
                    last = self._mm3(
                        T, pv,
                        self.qphi[:, kc * PD + ec * 128: kc * PD + ec * 128 + 128],
                        self.qplo[:, kc * PD + ec * 128: kc * PD + ec * 128 + 128],
                        self.Bhi[:, kc * 512:(kc + 1) * 512],
                        self.Blo[:, kc * 512:(kc + 1) * 512],
                        start=(kc == 0), stop=(kc == KC - 1))
                self.st.inc(last)
            self.wt(S)
            for ec in range(2):
                self.ss.inc(S.activation(qprime[:, ec * 512:(ec + 1) * 512],
                                         self.pB[:, ec * 512:(ec + 1) * 512], AF.Exp))
            self.ws(T)
            last = None
            for tt in range(TPC):
                pv = self.pB[:, 1024 + tt * 8: 1024 + tt * 8 + 1]
                for ec in range(2):
                    last = T.matmul(pv, qprime[:, ec * 512 + tt * 128: ec * 512 + (tt + 1) * 128],
                                    self.ones[:], start=(ec == 0), stop=(ec == 1))
            self.st.inc(last)
            self.wt(V)
            for tt in range(TPC):
                zc = self.zcol[:, tt:tt + 1]
                self.sv.inc(V.tensor_scalar_add(zc, self.pB[:, 1024 + tt * 8: 1024 + tt * 8 + 1], Z_EPS))
            self.wv(V)
            for tt in range(TPC):
                zc = self.zcol[:, tt:tt + 1]
                self.sv.inc(V.reciprocal(zc, zc))
            if c + 1 < self.n_chunks:
                self.emit_ln(c + 1, use_saved=True)
            num_done = {}
            for tt in range(TPC):
                for dh in range(2):
                    i = tt * 2 + dh
                    base = 1536 if i % 2 == 0 else 1024
                    pv = self.pB[:, base: base + 512]
                    if i < 2:
                        self.wv(T)
                    else:
                        self.wv(T, num_done[i - 2])
                    for ec in range(2):
                        last = T.matmul(pv, qprime[:, ec * 512 + tt * 128: ec * 512 + (tt + 1) * 128],
                                        (kv_s0 if ec == 0 else kv_s1)[:, dh * 512:(dh + 1) * 512],
                                        start=(ec == 0), stop=(ec == 1))
                    self.st.inc(last)
                    self.wt(V)
                    g = c * TPC + tt
                    dst = self.x_res[:, g * D + dh * 512: g * D + dh * 512 + 512]
                    self.sv.inc(V.scalar_tensor_tensor(dst, pv, self.zcol[:, tt:tt + 1],
                                                       dst, ALU.mult, ALU.add))
                    num_done[i] = self.sv.cnt
        self.wb_free_at = [self.st.cnt, self.st.cnt]

    def emit_store(self):
        nc = self.nc
        nc.sync.wait_ge(self.sv.h, self.sv.cnt)
        qtiles = self.n_tiles // 4
        for q in range(4):
            rows = self.t["o"][q * qtiles * 128:(q + 1) * qtiles * 128, :]
            dst = rows.rearrange("(t p) f -> p t f", p=128)
            src = self.x_res[:, q * qtiles * D:(q + 1) * qtiles * D]
            src = src.rearrange("p (t f) -> p t f", f=D)
            self.sout.inc(nc.sync.dma_start(out=dst, in_=src), 16)
        nc.sync.wait_ge(self.sout.h, self.sout.cnt)

    def emit_all(self):
        self.emit_loads()
        self.emit_prologue()
        for c in range(self.n_chunks):
            self.emit_ffn_chunk(c, ln_done=(c > 0))
        self.emit_attn()
        for c in range(self.n_chunks):
            self.emit_ffn_chunk(c, ln_done=(c > 0))
        self.emit_store()


def build_program(n_tok=4096):
    nc = bass.Bass("TRN2", target_bir_lowering=False, debug=False)
    b = Builder(nc, n_tok)
    b.emit_all()
    return nc


def prep_inputs(L0_attn_q_proj, L0_attn_k_proj, L0_ffn_P_w, L0_ffn_U1, L0_ffn_U2,
                L0_ffn_U3, L0_ffn_W1, L0_ffn_W2, L0_ffn_b2):
    c = np.ascontiguousarray
    f = np.float32
    return {
        "w1t": c(np.asarray(L0_ffn_W1, dtype=f).transpose(0, 2, 1)),
        "w2t": c(np.asarray(L0_ffn_W2, dtype=f).transpose(0, 2, 1)),
        "pwt": c(np.asarray(L0_ffn_P_w, dtype=f).T),
        "ut": c(np.concatenate([np.asarray(L0_ffn_U1, dtype=f).T,
                                np.asarray(L0_ffn_U2, dtype=f).T,
                                np.asarray(L0_ffn_U3, dtype=f).T], axis=1)),
        "b2": c(np.asarray(L0_ffn_b2, dtype=f)),
        "qp": c(np.asarray(L0_attn_q_proj, dtype=f)),
        "kp": c(np.asarray(L0_attn_k_proj, dtype=f)),
        "ident": np.eye(128, dtype=f),
    }


def kernel(L0_attn_x, L0_attn_q_proj, L0_attn_k_proj, L0_ffn_P_w, L0_ffn_U1,
           L0_ffn_U2, L0_ffn_U3, L0_ffn_W1, L0_ffn_W2, L0_ffn_b2):
    x = np.ascontiguousarray(np.asarray(L0_attn_x, dtype=np.float32))
    B = x.shape[0]
    shared = prep_inputs(L0_attn_q_proj, L0_attn_k_proj, L0_ffn_P_w, L0_ffn_U1,
                         L0_ffn_U2, L0_ffn_U3, L0_ffn_W1, L0_ffn_W2, L0_ffn_b2)
    nc = build_program(n_tok=x.shape[1])
    in_maps = [dict(shared, x=np.ascontiguousarray(x[c])) for c in range(B)]
    res = run_bass_kernel_spmd(nc, in_maps, core_ids=list(range(B)))
    return np.stack([res.results[c]["o"] for c in range(B)], axis=0)
